# version 03
# baseline (speedup 1.0000x reference)
import os
import numpy as np
from contextlib import ExitStack

import concourse.bass as bass
import concourse.mybir as mybir
from concourse.bass_utils import run_bass_kernel_spmd

F32 = mybir.dt.float32
FP16 = mybir.dt.float16
BF16 = mybir.dt.bfloat16
I16 = mybir.dt.int16
I32 = mybir.dt.int32
OP = mybir.AluOpType
AF = mybir.ActivationFunctionType
AX = mybir.AxisListType

S = 4096
D = 1024
DH = 4096
NT = 32
NCH = 16
EPS = 1e-5

ENGINES = ("sync", "vector", "scalar", "tensor", "gpsimd")


class Sem:
    __slots__ = ("h", "v", "name")

    def __init__(self, nc, es, name):
        self.h = es.enter_context(nc.semaphore(name))
        self.v = 0
        self.name = name

    def inc(self, n=1):
        self.v += n
        return (self, n, self.v)

    def now(self):
        return (self, self.v)


class Res:
    __slots__ = ("free",)

    def __init__(self):
        self.free = ()

    def wait(self):
        return list(self.free)

    def freed_by(self, *toks):
        self.free = tuple(toks)


class Prog:
    CHAINED = ("vector",)

    def __init__(self, nc, es):
        self.nc = nc
        self.es = es
        self.ops = {e: [] for e in ENGINES}
        self.chain = {e: Sem(nc, es, f"chain_{e}") for e in self.CHAINED}
        self.last_tok = {e: None for e in self.CHAINED}

    def sem(self, name):
        return Sem(self.nc, self.es, name)

    def op(self, engine, fn, wait=(), inc=(), chain=True):
        w = []
        for t in wait:
            if t is None:
                continue
            s, v = t
            if v > 0:
                w.append((s, v))
        inc = list(inc)
        if chain and engine in self.chain:
            lt = self.last_tok[engine]
            if lt is not None:
                w.append(lt)
            if not inc:
                inc = [self.chain[engine].inc()]
            self.last_tok[engine] = (inc[0][0], inc[0][2])
        self.ops[engine].append((fn, tuple(w), tuple(inc)))

    def barrier(self, name):
        s = self.sem(name)
        for e in ENGINES:
            self.op(e, lambda eng: eng.nop(), inc=[s.inc(1)], chain=False)
        for e in ENGINES:
            self.op(e, lambda eng: eng.nop(), wait=[(s, len(ENGINES))], chain=False)

    def replay(self, engine, eng):
        for fn, waits, incs in self.ops[engine]:
            for (s, v) in waits:
                eng.wait_ge(s.h, v)
            ins = fn(eng)
            gate = None
            for k, (s, n, cum) in enumerate(incs):
                if k == 0:
                    ins.then_inc(s.h, n)
                    gate = (s, cum)
                else:
                    if gate is not None:
                        eng.wait_ge(gate[0].h, gate[1])
                        gate = None
                    eng.nop().then_inc(s.h, n)


def build_nc(debug=False):
    PHASES = int(os.environ.get("KPHASES", "3"))
    nc = bass.Bass("TRN2", target_bir_lowering=False)

    dI = lambda n, sh, dt=F32: nc.dram_tensor(n, sh, dt, kind="ExternalInput").ap()
    dO = lambda n, sh, dt=F32: nc.dram_tensor(n, sh, dt, kind="ExternalOutput").ap()
    dS = lambda n, sh, dt=F32: nc.dram_tensor(
        n, sh, dt, kind="ExternalOutput" if debug else "Internal").ap()

    x_in = dI("x", [S, D])
    wq_d, wk_d, wv_d, wg_d, wo_d = (dI(n, [D, D]) for n in ("wq", "wk", "wv", "wg", "wo"))
    bq_d, bk_d, bv_d, bg_d, bo_d = (dI(n, [D]) for n in ("bq", "bk", "bv", "bg", "bo"))
    w1_d = dI("w1", [DH, D])
    cw_d = dI("cw", [DH, 3])
    cb_d = dI("cb", [DH])
    w2_d = dI("w2", [D, DH])
    out_d = dO("out", [S, D])

    x2_d = dS("x2", [S, D])
    xnth_d = dS("xnth", [NT, 128, D], FP16)
    xntl_d = dS("xntl", [NT, 128, D], FP16)
    w1q_d = dS("w1q", [32, 8, 128, 128], BF16)
    w2q_d = dS("w2q", [32, 128, D], BF16)
    kv_dbg = dO("kv_dbg", [128, 1024]) if debug else None
    dbg_d = dO("dbg", [128, 64]) if debug else None
    xq_dbg = dO("xq_dbg", [NT, 128, D], BF16) if debug else None
    hq_dbg = dO("hq_dbg", [NCH, 128, 32, 256], BF16) if debug else None
    am_dbg = dO("am_dbg", [NCH, 128, 8], F32) if debug else None

    es_g = ExitStack()
    with es_g:
        sbg = lambda n, sh, dt=F32: es_g.enter_context(nc.sbuf_tensor(n, sh, dt))
        p = Prog(nc, es_g)

        s_rg = p.sem("s_rg")
        s_stq = p.sem("s_stq")
        identf = sbg("identf", [128, 128])
        identb = sbg("identb", [128, 128], BF16)
        ones1h = sbg("ones1h", [1, 128], FP16)
        onesf = sbg("onesf", [128, 128])
        bqcol = sbg("bqcol", [128, 8])
        bgcol = sbg("bgcol", [128, 8])
        bo_r = sbg("bo_r", [1, 2 * D], FP16)
        w0col = sbg("w0col", [128, 32])
        w1col = sbg("w1col", [128, 32])
        w2col = sbg("w2col", [128, 32])
        cbcol = sbg("cbcol", [128, 32])
        wsc1 = sbg("wsc1", [128, 1])
        wsc2 = sbg("wsc2", [128, 1])
        m127_1 = sbg("m127_1", [128, 1])
        m127_2 = sbg("m127_2", [128, 1])
        kv_h = sbg("kv_h", [128, 1024], FP16)
        kv_l = sbg("kv_l", [128, 1024], FP16)
        statsA = sbg("statsA", [128, 32])
        trashA = sbg("trashA", [128, 1024], BF16)
        ceps = sbg("ceps", [128, 1])
        dbgb = sbg("dbgb", [128, 64]) if debug else None
        nc.const_aps.register(F32, EPS, ceps.ap()[:, 0:1])

        def emit_ln(idx, src_fn, sems, extra_wait_bn=(), extra_wait_rs=()):
            s_st, s_sq, s_rs = sems
            base = 8 * (idx % 4)
            st = lambda a, b: statsA[:, base + a:base + b]
            p.op("vector", lambda v: v.reduce_sum(out=st(0, 1), in_=src_fn(0, 1024), axis=AX.X),
                 wait=list(extra_wait_bn), inc=[s_st.inc()])
            t_sta = (s_st, s_st.v)
            p.op("scalar", lambda sE: sE.activation(out=trashA[:, :], in_=src_fn(0, 1024),
                                                    func=AF.Square, accum_out=st(1, 2)),
                 wait=list(extra_wait_bn), inc=[s_stq.inc()])
            t_stb = (s_stq, s_stq.v)
            p.op("vector", lambda v: v.tensor_scalar(out=st(2, 3), in0=st(0, 1),
                                                     scalar1=1.0 / 1024.0, scalar2=None,
                                                     op0=OP.mult), wait=[t_sta, t_stb], inc=[s_rg.inc()])
            p.op("vector", lambda v: v.tensor_scalar(out=st(3, 4), in0=st(2, 3),
                                                     scalar1=st(2, 3), scalar2=None,
                                                     op0=OP.mult),
                 wait=[(s_rg, s_rg.v)], inc=[s_rg.inc()])
            p.op("vector", lambda v: v.scalar_tensor_tensor(out=st(4, 5), in0=st(1, 2),
                                                            scalar=1.0 / 1024.0, in1=st(3, 4),
                                                            op0=OP.mult, op1=OP.subtract),
                 wait=[(s_rg, s_rg.v)], inc=[s_sq.inc()])
            p.op("scalar", lambda sE: sE.activation(out=st(5, 6), in_=st(4, 5),
                                                    func=AF.Sqrt, bias=EPS, scale=1.0),
                 wait=[(s_sq, s_sq.v)], inc=[s_sq.inc()])
            p.op("vector", lambda v: v.reciprocal(out=st(6, 7), in_=st(5, 6)),
                 wait=[(s_sq, s_sq.v)] + list(extra_wait_rs), inc=[s_rg.inc()])
            p.op("vector", lambda v: v.scalar_tensor_tensor(out=st(7, 8), in0=st(2, 3),
                                                            scalar=-1.0, in1=st(6, 7),
                                                            op0=OP.mult, op1=OP.mult),
                 wait=[(s_rg, s_rg.v)], inc=[s_rs.inc()])
            return (s_rs, s_rs.v), (st(6, 7), st(7, 8))

        s_prep = p.sem("s_prep")
        p.op("gpsimd", lambda g: g.iota(identf[:], [[1, 128]], channel_multiplier=0,
                                        allow_small_or_imprecise_dtypes=True))
        p.op("gpsimd", lambda g: g.iota(onesf[:], [[0, 128]], channel_multiplier=1,
                                        allow_small_or_imprecise_dtypes=True),
             inc=[s_prep.inc()])
        p.op("vector", lambda v: v.tensor_tensor(out=identf[:], in0=identf[:], in1=onesf[:],
                                                 op=OP.is_equal), wait=[(s_prep, 1)])
        p.op("vector", lambda v: v.tensor_copy(out=identb[:], in_=identf[:]))
        p.op("vector", lambda v: v.memset(onesf[:], 1.0))
        p.op("vector", lambda v: v.memset(ceps[:], EPS))
        if debug:
            p.op("vector", lambda v: v.memset(dbgb[:], 0.0))
        p.op("vector", lambda v: v.memset(ones1h[:], 1.0), inc=[s_prep.inc()])
        t_ident = (s_prep, s_prep.v)

        s_bld = p.sem("s_bld")
        p.op("sync", lambda e: e.dma_start(out=bqcol[:], in_=bq_d.rearrange("(c p) -> p c", p=128), allow_slow_non_contiguous=True),
             inc=[s_bld.inc(16)])
        p.op("sync", lambda e: e.dma_start(out=bgcol[:], in_=bg_d.rearrange("(c p) -> p c", p=128), allow_slow_non_contiguous=True),
             inc=[s_bld.inc(16)])
        for k, col in ((0, w0col), (1, w1col), (2, w2col)):
            p.op("sync", lambda e, k=k, col=col: e.dma_start(
                out=col[:], in_=cw_d[:, k].rearrange("(s p) -> p s", p=128), allow_slow_non_contiguous=True), inc=[s_bld.inc(16)])
        p.op("sync", lambda e: e.dma_start(out=cbcol[:], in_=cb_d.rearrange("(s p) -> p s", p=128), allow_slow_non_contiguous=True),
             inc=[s_bld.inc(16)])

        es1 = ExitStack()
        sb1 = lambda n, sh, dt=F32: es1.enter_context(nc.sbuf_tensor(n, sh, dt))
        ps1 = lambda n, sh, dt=F32: es1.enter_context(nc.psum_tensor(n, sh, dt))

        bk_r = sb1("bk_r", [1, 2 * D], FP16)
        bv_r = sb1("bv_r", [1, 2 * D], FP16)
        brow_st = sb1("brow_st", [1, 3 * D])
        wk_h = sb1("wk_h", [128, 8, D], FP16)
        wk_l = sb1("wk_l", [128, 8, D], FP16)
        wv_h = sb1("wv_h", [128, 8, D], FP16)
        wv_l = sb1("wv_l", [128, 8, D], FP16)
        wstage = sb1("wstage", [128, 2, D])

        p.op("sync", lambda e: e.dma_start(out=brow_st[:, 0:D], in_=bk_d[None, :]), inc=[s_bld.inc(16)])
        p.op("sync", lambda e: e.dma_start(out=brow_st[:, D:2 * D], in_=bv_d[None, :]), inc=[s_bld.inc(16)])
        p.op("sync", lambda e: e.dma_start(out=brow_st[:, 2 * D:3 * D], in_=bo_d[None, :]), inc=[s_bld.inc(16)])
        brows_tok = (s_bld, s_bld.v)
        for (off, dst) in ((0, bk_r), (D, bv_r), (2 * D, bo_r)):
            p.op("vector", lambda v, o=off, d=dst: v.tensor_copy(out=d[:, 0:D], in_=brow_st[:, o:o + D]),
                 wait=[brows_tok])
            p.op("vector", lambda v, o=off, d=dst: v.scalar_tensor_tensor(
                out=d[:, D:2 * D], in0=d[:, 0:D], scalar=-1.0, in1=brow_st[:, o:o + D],
                op0=OP.mult, op1=OP.add))
        p.op("vector", lambda v: v.nop(), inc=[s_prep.inc()])
        t_bias = (s_prep, s_prep.v)

        s_wld = [p.sem("s_wld0"), p.sem("s_wld1")]
        s_wsp = p.sem("s_wsp")
        wstage_free = [Res(), Res()]

        def emit_w_split(wd, wh, wl):
            for c in range(8):
                buf = c % 2
                p.op("sync", lambda e, c=c, buf=buf, wd=wd: e.dma_start(
                    out=wstage[:, buf, :], in_=wd[c * 128:(c + 1) * 128, :]),
                    wait=wstage_free[buf].wait(), inc=[s_wld[buf].inc(16)])
                tok_ld = (s_wld[buf], s_wld[buf].v)
                p.op("vector", lambda v, c=c, buf=buf, wh=wh: v.tensor_copy(
                    out=wh[:, c, :], in_=wstage[:, buf, :]), wait=[tok_ld])
                p.op("vector", lambda v, c=c, buf=buf, wh=wh, wl=wl: v.scalar_tensor_tensor(
                    out=wl[:, c, :], in0=wh[:, c, :], scalar=-1.0, in1=wstage[:, buf, :],
                    op0=OP.mult, op1=OP.add), inc=[s_wsp.inc()])
                wstage_free[buf].freed_by((s_wsp, s_wsp.v))

        emit_w_split(wk_d, wk_h, wk_l)
        emit_w_split(wv_d, wv_h, wv_l)
        t_wkv = (s_wsp, s_wsp.v)

        es0 = ExitStack()
        sb0 = lambda n, sh, dt=F32: es0.enter_context(nc.sbuf_tensor(n, sh, dt))
        ps0 = lambda n, sh, dt=F32: es0.enter_context(nc.psum_tensor(n, sh, dt))
        qstage = sb0("qstage", [128, 2, DH])
        qs2 = sb0("qs2", [128, DH], BF16)
        qpos = sb0("qpos", [128, DH], BF16)
        wqout = sb0("wqout", [128, 2, 1024], BF16)
        wabs = sb0("wabs", [128, 44])
        svals = sb0("svals", [1, 8])
        pmean = ps0("pmean", [1, 2])
        pbc = ps0("pbc", [128, 6])

        s_qld = [p.sem("s_qld0"), p.sem("s_qld1")]
        s_qrd = p.sem("s_qrd")
        qstage_free = [Res(), Res()]
        for c in range(32):
            buf = c % 2
            p.op("sync", lambda e, c=c, buf=buf: e.dma_start(
                out=qstage[:, buf, 0:D], in_=w1_d[c * 128:(c + 1) * 128, :]),
                wait=qstage_free[buf].wait(), inc=[s_qld[buf].inc(16)])
            tok = (s_qld[buf], s_qld[buf].v)
            p.op("vector", lambda v, c=c, buf=buf: v.reduce_sum(
                out=wabs[:, c:c + 1], in_=qstage[:, buf, 0:D], axis=AX.X,
                apply_absolute_value=True), wait=[tok], inc=[s_qrd.inc()])
            qstage_free[buf].freed_by((s_qrd, s_qrd.v))
        for c in range(8):
            buf = c % 2
            p.op("sync", lambda e, c=c, buf=buf: e.dma_start(
                out=qstage[:, buf, :], in_=w2_d[c * 128:(c + 1) * 128, :]),
                wait=qstage_free[buf].wait(), inc=[s_qld[buf].inc(16)])
            tok = (s_qld[buf], s_qld[buf].v)
            p.op("vector", lambda v, c=c, buf=buf: v.reduce_sum(
                out=wabs[:, 32 + c:33 + c], in_=qstage[:, buf, :], axis=AX.X,
                apply_absolute_value=True), wait=[tok], inc=[s_qrd.inc()])
            qstage_free[buf].freed_by((s_qrd, s_qrd.v))
        p.op("vector", lambda v: v.reduce_sum(out=wabs[:, 40:41], in_=wabs[:, 0:32], axis=AX.X),
             wait=[(s_qrd, s_qrd.v)], inc=[s_rg.inc()])
        p.op("vector", lambda v: v.reduce_sum(out=wabs[:, 41:42], in_=wabs[:, 32:40], axis=AX.X),
             wait=[(s_rg, s_rg.v)], inc=[s_prep.inc()])
        t_wsum = (s_prep, s_prep.v)
        p.op("tensor", lambda t: t.matmul(out=pmean[0:1, 0:1], lhsT=wabs[:, 40:41],
                                          rhs=onesf[:, 0:1], start=True, stop=True),
             wait=[t_wsum, t_ident])
        p.op("tensor", lambda t: t.matmul(out=pmean[0:1, 1:2], lhsT=wabs[:, 41:42],
                                          rhs=onesf[:, 0:1], start=True, stop=True),
             inc=[s_prep.inc()])
        t_pm = (s_prep, s_prep.v)
        p.op("vector", lambda v: v.tensor_scalar(
            out=svals[0:1, 0:2], in0=pmean[0:1, 0:2], scalar1=1.0 / (DH * D), scalar2=1e-5,
            op0=OP.mult, op1=OP.max), wait=[t_pm], inc=[s_rg.inc()])
        p.op("vector", lambda v: v.reciprocal(out=svals[0:1, 2:4], in_=svals[0:1, 0:2]),
             wait=[(s_rg, s_rg.v)], inc=[s_rg.inc()])
        p.op("vector", lambda v: v.tensor_scalar(
            out=svals[0:1, 4:6], in0=svals[0:1, 0:2], scalar1=1.0 / 127.0, scalar2=None,
            op0=OP.mult), wait=[(s_rg, s_rg.v)], inc=[s_prep.inc()])
        t_sv = (s_prep, s_prep.v)
        for jj, cc in enumerate((2, 3, 4, 5)):
            p.op("tensor", lambda t, cc=cc: t.matmul(
                out=pbc[:, cc - 2:cc - 1], lhsT=onesf[0:1, :], rhs=svals[0:1, cc:cc + 1],
                start=True, stop=True),
                wait=[t_sv] if jj == 0 else [],
                inc=[s_prep.inc()] if jj == 3 else [])
        t_pbc = (s_prep, s_prep.v)
        p.op("vector", lambda v: v.tensor_copy(out=wsc1[:, :], in_=pbc[:, 0:1]), wait=[t_pbc])
        p.op("vector", lambda v: v.tensor_copy(out=wsc2[:, :], in_=pbc[:, 1:2]))
        p.op("vector", lambda v: v.tensor_copy(out=m127_1[:, :], in_=pbc[:, 2:3]))
        p.op("vector", lambda v: v.tensor_copy(out=m127_2[:, :], in_=pbc[:, 3:4]),
             inc=[s_prep.inc()])
        t_wsc = (s_prep, s_prep.v)
        if debug:
            pass
            p.op("vector", lambda v: v.tensor_copy(out=dbgb[:, 0:1], in_=wsc1[:, :]), wait=[t_wsc])
            p.op("vector", lambda v: v.tensor_copy(out=dbgb[:, 1:2], in_=m127_1[:, :]))
            p.op("vector", lambda v: v.tensor_copy(out=dbgb[:, 2:3], in_=wsc2[:, :]))
            p.op("vector", lambda v: v.tensor_copy(out=dbgb[:, 3:4], in_=m127_2[:, :]))
            p.op("vector", lambda v: v.tensor_copy(out=dbgb[0:1, 8:14], in_=svals[0:1, 0:6]))
            p.op("vector", lambda v: v.tensor_copy(out=dbgb[0:1, 14:16], in_=pmean[0:1, 0:2]))
            p.op("vector", lambda v: v.tensor_copy(out=dbgb[:, 16:18], in_=wabs[:, 40:42]),
                 inc=[s_prep.inc()])
            t_dbg0 = (s_prep, s_prep.v)

        ptq = ps0("ptq", [128, 1024], BF16)
        s_q2 = p.sem("s_q2")
        s_qmm = p.sem("s_qmm")
        s_qev = p.sem("s_qev")
        s_qst = [p.sem("s_qst0"), p.sem("s_qst1")]
        qst_n = [0]
        qs2_free = Res()
        ptq_free = Res()
        wqout_free = [Res(), Res()]

        def emit_quant(src_ap, n_chunks, width, wsc, which):
            for c in range(n_chunks):
                buf = c % 2
                p.op("sync", lambda e, c=c, buf=buf, src=src_ap, w=width: e.dma_start(
                    out=qstage[:, buf, 0:w], in_=src[c * 128:(c + 1) * 128, :]),
                    wait=qstage_free[buf].wait(), inc=[s_qld[buf].inc(16)])
                tok_ld = (s_qld[buf], s_qld[buf].v)
                p.op("vector", lambda v, buf=buf, w=width, wsc=wsc: v.tensor_scalar(
                    out=qpos[:, 0:w], in0=qstage[:, buf, 0:w], scalar1=wsc[:, 0:1],
                    scalar2=0.5, op0=OP.mult, op1=OP.is_gt),
                    wait=[tok_ld, t_wsc] + qs2_free.wait())
                p.op("vector", lambda v, buf=buf, w=width, wsc=wsc: v.tensor_scalar(
                    out=qs2[:, 0:w], in0=qstage[:, buf, 0:w], scalar1=wsc[:, 0:1],
                    scalar2=-0.5, op0=OP.mult, op1=OP.is_lt), inc=[s_qrd.inc()])
                qstage_free[buf].freed_by((s_qrd, s_qrd.v))
                p.op("vector", lambda v, w=width: v.tensor_tensor(
                    out=qs2[:, 0:w], in0=qpos[:, 0:w], in1=qs2[:, 0:w], op=OP.subtract),
                    inc=[s_q2.inc()])
                t_q2 = (s_q2, s_q2.v)
                ngrp = width // 1024
                for g in range(ngrp):
                    for tt in range(8):
                        p.op("tensor", lambda t, g=g, tt=tt: t.transpose(
                            out=ptq[:, tt * 128:(tt + 1) * 128],
                            in_=qs2[:, g * 1024 + tt * 128:g * 1024 + (tt + 1) * 128],
                            identity=identb[:]),
                            wait=([t_q2, t_ident] + ptq_free.wait()) if tt == 0 else [],
                            inc=[s_qmm.inc()] if tt == 7 else [])
                    t_tr = (s_qmm, s_qmm.v)
                    qb = qst_n[0] % 2
                    qst_n[0] += 1
                    p.op("scalar", lambda sE, qb=qb: sE.activation(
                        out=wqout[:, qb, :], in_=ptq[:, :], func=AF.Identity),
                        wait=[t_tr] + wqout_free[qb].wait(), inc=[s_qev.inc()])
                    ptq_free.freed_by((s_qev, s_qev.v))
                    t_ev = (s_qev, s_qev.v)
                    if which == "w1":
                        p.op("gpsimd", lambda gE, c=c, qb=qb: gE.dma_start(
                            out=w1q_d[c].transpose([1, 0, 2]),
                            in_=wqout[:, qb, :].rearrange("p (t c) -> p t c", t=8)),
                            wait=[t_ev], inc=[s_qst[qb].inc(16)])
                    else:
                        p.op("gpsimd", lambda gE, c=c, g=g, qb=qb: gE.dma_start(
                            out=w2q_d[8 * g:8 * (g + 1), :, c * 128:(c + 1) * 128].transpose([1, 0, 2]),
                            in_=wqout[:, qb, :].rearrange("p (t c) -> p t c", t=8)),
                            wait=[t_ev], inc=[s_qst[qb].inc(16)])
                    wqout_free[qb].freed_by((s_qst[qb], s_qst[qb].v))
                qs2_free.freed_by((s_qmm, s_qmm.v))

        emit_quant(w1_d, 32, D, wsc1, "w1")
        emit_quant(w2_d, 8, DH, wsc2, "w2")
        t_wq_stored0 = (s_qst[0], s_qst[0].v)
        t_wq_stored1 = (s_qst[1], s_qst[1].v)

        p.op("sync", lambda e: e.nop(), wait=[(s_qld[0], s_qld[0].v), (s_qld[1], s_qld[1].v)])
        p.op("gpsimd", lambda g: g.nop(), wait=[(s_qst[0], s_qst[0].v), (s_qst[1], s_qst[1].v)])
        p.op("sync", lambda e: e.nop(), wait=[(s_wld[0], s_wld[0].v), (s_wld[1], s_wld[1].v), (s_bld, s_bld.v)])
        p.barrier("b0")
        es0.close()

        xin = sb1("xin", [128, 2, D])
        xnb = sb1("xnb", [128, 2, D])
        kvacc = sb1("kvacc", [128, 1024])
        ktmp = sb1("ktmp", [128, D])
        kf32 = sb1("kf32", [128, D])
        k_h = sb1("k_h", [128, D], FP16)
        k_l = sb1("k_l", [128, D], FP16)
        v_h = sb1("v_h", [128, D], FP16)
        v_l = sb1("v_l", [128, D], FP16)
        xnth_b = sb1("xnth_b", [128, 2, D], FP16)
        xntl_b = sb1("xntl_b", [128, 2, D], FP16)
        ptr = ps1("ptr", [128, 1024])
        pk = ps1("pk", [128, 1024])
        pv = ps1("pv", [128, 1024])
        pkv = ps1("pkv", [128, 1024])

        s_x = [p.sem("s_x0"), p.sem("s_x1")]
        s_st1 = p.sem("s_st1")
        s_sq1 = p.sem("s_sq1")
        s_rs1 = p.sem("s_rs1")
        s_xn = p.sem("s_xn")
        s_tr = p.sem("s_tr")
        s_hev = p.sem("s_hev")
        s_lev = p.sem("s_lev")
        s_xst = [p.sem("s_xst0"), p.sem("s_xst1")]
        s_kvp = p.sem("s_kvp")
        s_kch = p.sem("s_kch")
        s_vch = p.sem("s_vch")
        s_kvm = p.sem("s_kvm")
        s_kva = p.sem("s_kva")

        xin_free = [Res(), Res()]
        xnb_free = [Res(), Res()]
        xT_free = [Res(), Res()]
        ptr_free = Res()
        pk_free = Res()
        pv_free = Res()
        pkv_free = Res()
        ktmp_free = Res()
        stats_free = [Res(), Res(), Res(), Res()]

        for i in range(NT):
            buf = i % 2
            p.op("sync", lambda e, i=i, buf=buf: e.dma_start(
                out=xin[:, buf, :], in_=x_in[i * 128:(i + 1) * 128, :]),
                wait=xin_free[buf].wait(), inc=[s_x[buf].inc(16)])
            t_x = (s_x[buf], s_x[buf].v)
            t_rs, (rstd_ap, nmr_ap) = emit_ln(
                i, lambda lo, hi, buf=buf: xin[:, buf, lo:hi],
                (s_st1, s_sq1, s_rs1),
                extra_wait_bn=[t_x] + stats_free[i % 4].wait())
            p.op("scalar", lambda sE, buf=buf, r=rstd_ap, m=nmr_ap: sE.activation(
                out=xnb[:, buf, :], in_=xin[:, buf, :], func=AF.Identity,
                bias=m, scale=r),
                wait=[t_rs] + xnb_free[buf].wait(), inc=[s_xn.inc()])
            xin_free[buf].freed_by((s_xn, s_xn.v))
            t_xn = (s_xn, s_xn.v)
            for c in range(8):
                p.op("tensor", lambda t, c=c, buf=buf: t.transpose(
                    out=ptr[:, c * 128:(c + 1) * 128], in_=xnb[:, buf, c * 128:(c + 1) * 128],
                    identity=identf[:]),
                    wait=([t_xn, t_ident] + ptr_free.wait()) if c == 0 else [],
                    inc=[s_tr.inc()] if c == 7 else [])
            xnb_free[buf].freed_by((s_tr, s_tr.v))
            t_tr = (s_tr, s_tr.v)
            p.op("scalar", lambda sE, buf=buf: sE.activation(
                out=xnth_b[:, buf, :], in_=ptr[:, :], func=AF.Identity),
                wait=[t_tr] + xT_free[buf].wait(), inc=[s_hev.inc()])
            t_h = (s_hev, s_hev.v)
            p.op("vector", lambda v, buf=buf: v.scalar_tensor_tensor(
                out=xntl_b[:, buf, :], in0=xnth_b[:, buf, :], scalar=-1.0, in1=ptr[:, :],
                op0=OP.mult, op1=OP.add),
                wait=[t_tr, t_h] + xT_free[buf].wait(), inc=[s_lev.inc()])
            t_l = (s_lev, s_lev.v)
            ptr_free.freed_by(t_h, t_l)
            stats_free[i % 4].freed_by(t_xn)
            p.op("gpsimd", lambda g, i=i, buf=buf: g.dma_start(
                out=xnth_d[i], in_=xnth_b[:, buf, :]), wait=[t_h], inc=[s_xst[buf].inc(16)])
            p.op("gpsimd", lambda g, i=i, buf=buf: g.dma_start(
                out=xntl_d[i], in_=xntl_b[:, buf, :]), wait=[t_l], inc=[s_xst[buf].inc(16)])
            t_xst = (s_xst[buf], s_xst[buf].v)

            def kv_proj(t, buf=buf):
                last = None
                for (pp, wh, wl, br) in ((pk, wk_h, wk_l, bk_r), (pv, wv_h, wv_l, bv_r)):
                    for half in range(2):
                        o = pp[:, half * 512:(half + 1) * 512]
                        first = True
                        for (xs, ws) in ((xnth_b, wh), (xnth_b, wl), (xntl_b, wh)):
                            for c in range(8):
                                last = t.matmul(
                                    out=o, lhsT=xs[:, buf, c * 128:(c + 1) * 128],
                                    rhs=ws[:, c, half * 512:(half + 1) * 512],
                                    start=first, stop=False)
                                first = False
                        last = t.matmul(out=o, lhsT=ones1h[:, :],
                                        rhs=br[0:1, half * 512:half * 512 + 512],
                                        start=False, stop=False)
                        last = t.matmul(out=o, lhsT=ones1h[:, :],
                                        rhs=br[0:1, D + half * 512:D + half * 512 + 512],
                                        start=False, stop=True)
                return last
            p.op("tensor", kv_proj,
                 wait=[t_h, t_l, t_wkv, t_bias] + pk_free.wait() + pv_free.wait(),
                 inc=[s_kvp.inc()])
            xT_free[buf].freed_by((s_kvp, s_kvp.v), t_xst)
            t_kvp = (s_kvp, s_kvp.v)
            p.op("vector", lambda v: v.tensor_scalar(out=ktmp[:, :], in0=pk[:, :], scalar1=0.0,
                                                     scalar2=None, op0=OP.min),
                 wait=[t_kvp] + ktmp_free.wait(), inc=[s_kch.inc()])
            p.op("scalar", lambda sE: sE.activation(out=ktmp[:, :], in_=ktmp[:, :], func=AF.Exp),
                 wait=[(s_kch, s_kch.v)], inc=[s_kch.inc()])
            p.op("vector", lambda v: v.scalar_tensor_tensor(out=kf32[:, :], in0=pk[:, :],
                                                            scalar=0.0, in1=ktmp[:, :],
                                                            op0=OP.max, op1=OP.add),
                 wait=[(s_kch, s_kch.v)], inc=[s_kch.inc()])
            pk_free.freed_by((s_kch, s_kch.v))
            ktmp_free.freed_by((s_kch, s_kch.v))
            p.op("vector", lambda v: v.tensor_copy(out=k_h[:, :], in_=kf32[:, :]))
            p.op("vector", lambda v: v.scalar_tensor_tensor(out=k_l[:, :], in0=k_h[:, :],
                                                            scalar=-1.0, in1=kf32[:, :],
                                                            op0=OP.mult, op1=OP.add),
                 inc=[s_kch.inc()])
            t_k = (s_kch, s_kch.v)
            p.op("scalar", lambda sE: sE.activation(out=v_h[:, :], in_=pv[:, :], func=AF.Identity),
                 wait=[t_kvp] + ([] if i == 0 else []), inc=[s_vch.inc()])
            p.op("vector", lambda v: v.scalar_tensor_tensor(out=v_l[:, :], in0=v_h[:, :],
                                                            scalar=-1.0, in1=pv[:, :],
                                                            op0=OP.mult, op1=OP.add),
                 wait=[(s_vch, s_vch.v)], inc=[s_vch.inc()])
            pv_free.freed_by((s_vch, s_vch.v))
            t_v = (s_vch, s_vch.v)

            def kv_mm(t):
                last = None
                for pr in range(8):
                    sl = slice(pr * 128, (pr + 1) * 128)
                    last = t.matmul(out=pkv[:, sl], lhsT=k_h[:, sl], rhs=v_h[:, sl],
                                    start=True, stop=False)
                    last = t.matmul(out=pkv[:, sl], lhsT=k_h[:, sl], rhs=v_l[:, sl],
                                    start=False, stop=False)
                    last = t.matmul(out=pkv[:, sl], lhsT=k_l[:, sl], rhs=v_h[:, sl],
                                    start=False, stop=True)
                return last
            p.op("tensor", kv_mm, wait=[t_k, t_v] + pkv_free.wait(), inc=[s_kvm.inc()])
            t_kvmm = (s_kvm, s_kvm.v)
            if i == 0:
                p.op("vector", lambda v: v.tensor_copy(out=kvacc[:, :], in_=pkv[:, :]),
                     wait=[t_kvmm], inc=[s_kva.inc()])
            else:
                p.op("vector", lambda v: v.scalar_tensor_tensor(
                    out=kvacc[:, :], in0=pkv[:, :], scalar=1.0, in1=kvacc[:, :],
                    op0=OP.mult, op1=OP.add), wait=[t_kvmm], inc=[s_kva.inc()])
            pkv_free.freed_by((s_kva, s_kva.v))
            ktmp_free.freed_by((s_kva, s_kva.v))

        if debug:
            p.op("vector", lambda v: v.tensor_copy(out=dbgb[:, 32:64], in_=statsA[:, 0:32]),
                 wait=[(s_rs1, 1)], inc=[s_prep.inc()])
            p.op("gpsimd", lambda g: g.dma_start(out=dbg_d, in_=dbgb[:, :]),
                 wait=[(s_kva, s_kva.v), (s_prep, s_prep.v)], inc=[s_qst[0].inc(16)])
        p.op("vector", lambda v: v.tensor_copy(out=kv_h[:, :], in_=kvacc[:, :]),
             wait=[(s_kva, s_kva.v)])
        p.op("vector", lambda v: v.scalar_tensor_tensor(out=kv_l[:, :], in0=kv_h[:, :],
                                                        scalar=-1.0, in1=kvacc[:, :],
                                                        op0=OP.mult, op1=OP.add),
             inc=[s_prep.inc()])
        t_kv = (s_prep, s_prep.v)
        if debug:
            p.op("gpsimd", lambda g: g.dma_start(out=kv_dbg, in_=kvacc[:, :]),
                 wait=[t_kv], inc=[s_qst[0].inc(16)])
            p.op("gpsimd", lambda g: g.nop(), wait=[(s_qst[0], s_qst[0].v)])

        p.op("gpsimd", lambda g: g.nop(),
             wait=[(s_xst[0], s_xst[0].v), (s_xst[1], s_xst[1].v),
                   (s_qst[0], s_qst[0].v), (s_qst[1], s_qst[1].v)])
        p.op("sync", lambda e: e.nop(), wait=[(s_x[0], s_x[0].v), (s_x[1], s_x[1].v)])
        p.barrier("b1")
        es1.close()
        if PHASES < 2:
            for e in ENGINES:
                p.op(e, lambda eng: eng.nop(),
                     wait=[(s_xst[0], s_xst[0].v), (s_xst[1], s_xst[1].v)])
            for e in ENGINES:
                p.op(e, lambda eng: eng.nop(),
                     wait=[(s_qst[0], s_qst[0].v), (s_qst[1], s_qst[1].v)])
            return _finish(nc, es_g, p)

        es2 = ExitStack()
        sb2 = lambda n, sh, dt=F32: es2.enter_context(nc.sbuf_tensor(n, sh, dt))
        ps2 = lambda n, sh, dt=F32: es2.enter_context(nc.psum_tensor(n, sh, dt))

        wq_h = sb2("wq_h", [128, 8, D], FP16)
        wq_l = sb2("wq_l", [128, 8, D], FP16)
        wg_h = sb2("wg_h", [128, 8, D], FP16)
        wg_l = sb2("wg_l", [128, 8, D], FP16)
        wo_h = sb2("wo_h", [128, 8, D], FP16)
        wo_l = sb2("wo_l", [128, 8, D], FP16)
        wstage2 = sb2("wstage2", [128, 2, D])

        wstage_free = [Res(), Res()]

        def emit_w_split2(wd, wh, wl):
            for c in range(8):
                buf = c % 2
                p.op("sync", lambda e, c=c, buf=buf, wd=wd: e.dma_start(
                    out=wstage2[:, buf, :], in_=wd[c * 128:(c + 1) * 128, :]),
                    wait=wstage_free[buf].wait(), inc=[s_wld[buf].inc(16)])
                tok_ld = (s_wld[buf], s_wld[buf].v)
                p.op("vector", lambda v, c=c, buf=buf, wh=wh: v.tensor_copy(
                    out=wh[:, c, :], in_=wstage2[:, buf, :]), wait=[tok_ld])
                p.op("vector", lambda v, c=c, buf=buf, wh=wh, wl=wl: v.scalar_tensor_tensor(
                    out=wl[:, c, :], in0=wh[:, c, :], scalar=-1.0, in1=wstage2[:, buf, :],
                    op0=OP.mult, op1=OP.add), inc=[s_wsp.inc()])
                wstage_free[buf].freed_by((s_wsp, s_wsp.v))

        if os.environ.get("KEXIT") == "preW":
            for e in ENGINES:
                p.op(e, lambda eng: eng.nop(), wait=[(s_xst[0], s_xst[0].v), (s_xst[1], s_xst[1].v)])
            for e in ENGINES:
                p.op(e, lambda eng: eng.nop(), wait=[(s_qst[0], s_qst[0].v), (s_qst[1], s_qst[1].v)])
            es2.close()
            return _finish(nc, es_g, p)
        if int(os.environ.get("KP2W", "1")):
            emit_w_split2(wq_d, wq_h, wq_l)
            emit_w_split2(wg_d, wg_h, wg_l)
            emit_w_split2(wo_d, wo_h, wo_l)
        t_wqgo = (s_wsp, s_wsp.v)

        xh_b = sb2("xh_b", [128, 2, 2, D], FP16)
        xl_b = sb2("xl_b", [128, 2, 2, D], FP16)
        qf32 = sb2("qf32", [128, 8, 256])
        qtmp = sb2("qtmp", [128, 256])
        q_hb = sb2("q_hb", [128, 8, 256], FP16)
        q_lb = sb2("q_lb", [128, 8, 256], FP16)
        gT = sb2("gT", [128, 8, 256])
        gav_h = sb2("gav_h", [128, 8, 256], FP16)
        gav_l = sb2("gav_l", [128, 8, 256], FP16)
        xr_b = sb2("xr_b", [128, 2, D])
        ot_b = sb2("ot_b", [128, D])
        x2_b = sb2("x2_b", [128, 2, D])
        pq2 = [ps2("pq0", [128, 256]), ps2("pq1", [128, 256])]
        pg2 = [ps2("pg0", [128, 256]), ps2("pg1", [128, 256])]
        pa2 = [ps2("pa0", [128, 256]), ps2("pa1", [128, 256])]
        po = ps2("po", [128, 1024])

        s_x2ld = [p.sem("s_x2ld0"), p.sem("s_x2ld1")]
        s_qmm2 = p.sem("s_qmm2")
        s_qch = p.sem("s_qch")
        s_gmm = p.sem("s_gmm")
        s_gch = p.sem("s_gch")
        s_amm = p.sem("s_amm")
        s_gav = p.sem("s_gav")
        s_omm = p.sem("s_omm")
        s_ost = p.sem("s_ost")
        s_osq = p.sem("s_osq")
        s_ors = p.sem("s_ors")
        s_oap = p.sem("s_oap")
        s_xr = [p.sem("s_xr0"), p.sem("s_xr1")]
        s_x2w = p.sem("s_x2w")
        s_x2s = [p.sem("s_x2s0"), p.sem("s_x2s1")]

        xTb_free = [Res(), Res()]
        pq_free = [Res(), Res()]
        pg_free = [Res(), Res()]
        pa_free = [Res(), Res()]
        po_free = Res()
        qtmp_free = Res()
        qf_free = [Res() for _ in range(8)]
        qhl_free = [Res() for _ in range(8)]
        gT_free = [Res() for _ in range(8)]
        gav_free = [Res() for _ in range(8)]
        xr_free = [Res(), Res()]
        otb_free = Res()
        x2b_free = [Res(), Res()]
        stats_free = [Res(), Res(), Res(), Res()]

        KP2N = int(os.environ.get("KP2N", str(NCH)))
        KP2SUB = int(os.environ.get("KP2SUB", "5"))
        for j in range(KP2N):
            buf = j % 2
            p.op("sync", lambda e, j=j, buf=buf: e.dma_start(
                out=xh_b[:, buf, :, :], in_=xnth_d[2 * j:2 * j + 2].transpose([1, 0, 2])),
                wait=xTb_free[buf].wait() + [(s_xst[0], 32 * NT // 2), (s_xst[1], 32 * NT // 2)], inc=[s_x2ld[buf].inc(16)])
            p.op("sync", lambda e, j=j, buf=buf: e.dma_start(
                out=xl_b[:, buf, :, :], in_=xntl_d[2 * j:2 * j + 2].transpose([1, 0, 2])),
                inc=[s_x2ld[buf].inc(16)])
            t_xT = (s_x2ld[buf], s_x2ld[buf].v)
            for fo in range(8):
                pq = pq2[fo % 2]
                def q_mm(t, fo=fo, buf=buf, pq=pq):
                    last = None
                    passes = ((xh_b, wq_h), (xl_b, wq_h), (xh_b, wq_l))
                    for pi, (xs, ws) in enumerate(passes):
                        for c in range(8):
                            last = t.matmul(
                                out=pq[:, :], lhsT=ws[:, c, fo * 128:(fo + 1) * 128],
                                rhs=xs[:, buf, :, c * 128:(c + 1) * 128],
                                start=(pi == 0 and c == 0), stop=(pi == 2 and c == 7))
                    return last
                p.op("tensor", q_mm, wait=[t_xT, t_wqgo] + pq_free[fo % 2].wait(), inc=[s_qmm2.inc()])
                t_qmm = (s_qmm2, s_qmm2.v)
                p.op("vector", lambda v, fo=fo, pq=pq: v.tensor_scalar(
                    out=qtmp[:, :], in0=pq[:, :], scalar1=bqcol[:, fo:fo + 1], scalar2=0.0,
                    op0=OP.add, op1=OP.min),
                    wait=[t_qmm] + qtmp_free.wait(), inc=[s_qch.inc()])
                p.op("scalar", lambda sE: sE.activation(out=qtmp[:, :], in_=qtmp[:, :], func=AF.Exp),
                     wait=[(s_qch, s_qch.v)], inc=[s_qch.inc()])
                p.op("vector", lambda v, fo=fo, pq=pq: v.tensor_scalar(
                    out=qf32[:, fo, :], in0=pq[:, :], scalar1=bqcol[:, fo:fo + 1], scalar2=0.0,
                    op0=OP.add, op1=OP.max),
                    wait=[(s_qch, s_qch.v)] + qf_free[fo].wait(), inc=[s_qch.inc()])
                pq_free[fo % 2].freed_by((s_qch, s_qch.v))
                p.op("vector", lambda v, fo=fo: v.tensor_tensor(
                    out=qf32[:, fo, :], in0=qf32[:, fo, :], in1=qtmp[:, :], op=OP.add),
                    inc=[s_qch.inc()])
                qtmp_free.freed_by((s_qch, s_qch.v))
                p.op("vector", lambda v, fo=fo: v.tensor_copy(out=q_hb[:, fo, :], in_=qf32[:, fo, :]),
                     wait=qhl_free[fo].wait())
                p.op("vector", lambda v, fo=fo: v.scalar_tensor_tensor(
                    out=q_lb[:, fo, :], in0=q_hb[:, fo, :], scalar=-1.0, in1=qf32[:, fo, :],
                    op0=OP.mult, op1=OP.add), inc=[s_qch.inc()])
                qf_free[fo].freed_by((s_qch, s_qch.v))
            t_q = (s_qch, s_qch.v)
            for fo in range(8 if KP2SUB >= 2 else 0):
                pg = pg2[fo % 2]
                def g_mm(t, fo=fo, buf=buf, pg=pg):
                    last = None
                    passes = ((xh_b, wg_h), (xl_b, wg_h), (xh_b, wg_l))
                    for pi, (xs, ws) in enumerate(passes):
                        for c in range(8):
                            last = t.matmul(
                                out=pg[:, :], lhsT=ws[:, c, fo * 128:(fo + 1) * 128],
                                rhs=xs[:, buf, :, c * 128:(c + 1) * 128],
                                start=(pi == 0 and c == 0), stop=(pi == 2 and c == 7))
                    return last
                p.op("tensor", g_mm, wait=[t_xT, t_wqgo] + pg_free[fo % 2].wait(), inc=[s_gmm.inc()])
                t_gmm = (s_gmm, s_gmm.v)
                p.op("scalar", lambda sE, fo=fo, pg=pg: sE.activation(
                    out=gT[:, fo, :], in_=pg[:, :], func=AF.Sigmoid, bias=bgcol[:, fo:fo + 1],
                    scale=1.0),
                    wait=[t_gmm] + gT_free[fo].wait(), inc=[s_gch.inc()])
                pg_free[fo % 2].freed_by((s_gch, s_gch.v))
            xTb_free[buf].freed_by((s_qmm2, s_qmm2.v), (s_gmm, s_gmm.v))
            for pr in range(8 if KP2SUB >= 3 else 0):
                pa = pa2[pr % 2]
                def a_mm(t, pr=pr, pa=pa):
                    c0 = pr * 128
                    t.matmul(out=pa[0:64, :], lhsT=kv_h[0:64, c0:c0 + 64],
                             rhs=q_hb[0:64, pr, :], start=True, stop=False)
                    t.matmul(out=pa[0:64, :], lhsT=kv_h[0:64, c0:c0 + 64],
                             rhs=q_lb[0:64, pr, :], start=False, stop=False)
                    t.matmul(out=pa[0:64, :], lhsT=kv_l[0:64, c0:c0 + 64],
                             rhs=q_hb[0:64, pr, :], start=False, stop=True)
                    t.matmul(out=pa[64:128, :], lhsT=kv_h[64:128, c0 + 64:c0 + 128],
                             rhs=q_hb[64:128, pr, :], start=True, stop=False,
                             tile_position=(64, 64))
                    t.matmul(out=pa[64:128, :], lhsT=kv_h[64:128, c0 + 64:c0 + 128],
                             rhs=q_lb[64:128, pr, :], start=False, stop=False,
                             tile_position=(64, 64))
                    return t.matmul(out=pa[64:128, :], lhsT=kv_l[64:128, c0 + 64:c0 + 128],
                                    rhs=q_hb[64:128, pr, :], start=False, stop=True,
                                    tile_position=(64, 64))
                p.op("tensor", a_mm, wait=[t_q, t_kv] + pa_free[pr % 2].wait(), inc=[s_amm.inc()])
                t_amm = (s_amm, s_amm.v)
                p.op("vector", lambda v, pr=pr, pa=pa: v.scalar_tensor_tensor(
                    out=qf32[:, pr, :], in0=pa[:, :], scalar=1.0, in1=gT[:, pr, :],
                    op0=OP.mult, op1=OP.mult),
                    wait=[t_amm, (s_gch, s_gch.v)] + qf_free[pr].wait(), inc=[s_gav.inc()])
                pa_free[pr % 2].freed_by((s_gav, s_gav.v))
                gT_free[pr].freed_by((s_gav, s_gav.v))
                qhl_free[pr].freed_by((s_amm, s_amm.v))
                p.op("vector", lambda v, pr=pr: v.tensor_copy(
                    out=gav_h[:, pr, :], in_=qf32[:, pr, :]),
                    wait=gav_free[pr].wait())
                p.op("vector", lambda v, pr=pr: v.scalar_tensor_tensor(
                    out=gav_l[:, pr, :], in0=gav_h[:, pr, :], scalar=-1.0, in1=qf32[:, pr, :],
                    op0=OP.mult, op1=OP.add), inc=[s_gav.inc()])
                qf_free[pr].freed_by((s_gav, s_gav.v))
            t_gav = (s_gav, s_gav.v)
            for tl in range(2 if KP2SUB >= 4 else 0):
                i = 2 * j + tl
                ib = i % 2
                p.op("sync", lambda e, i=i, ib=ib: e.dma_start(
                    out=xr_b[:, ib, :], in_=x_in[i * 128:(i + 1) * 128, :]),
                    wait=xr_free[ib].wait(), inc=[s_xr[ib].inc(16)])
                t_xr = (s_xr[ib], s_xr[ib].v)

                def o_mm(t, tl=tl):
                    last = None
                    for half in range(2):
                        o = po[:, half * 512:(half + 1) * 512]
                        first = True
                        for (gs, ws) in ((gav_h, wo_h), (gav_h, wo_l), (gav_l, wo_h)):
                            for c in range(8):
                                last = t.matmul(
                                    out=o, lhsT=gs[:, c, tl * 128:(tl + 1) * 128],
                                    rhs=ws[:, c, half * 512:(half + 1) * 512],
                                    start=first, stop=False)
                                first = False
                        last = t.matmul(out=o, lhsT=ones1h[:, :],
                                        rhs=bo_r[0:1, half * 512:half * 512 + 512],
                                        start=False, stop=False)
                        last = t.matmul(out=o, lhsT=ones1h[:, :],
                                        rhs=bo_r[0:1, D + half * 512:D + half * 512 + 512],
                                        start=False, stop=True)
                    return last
                p.op("tensor", o_mm, wait=[t_gav, t_bias] + po_free.wait(), inc=[s_omm.inc()])
                t_omm = (s_omm, s_omm.v)
                if tl == 1:
                    for pr in range(8):
                        gav_free[pr].freed_by(t_omm)
                p.op("scalar", lambda sE: sE.activation(
                    out=ot_b[:, :], in_=po[:, :], func=AF.Identity),
                    wait=[t_omm] + otb_free.wait(), inc=[s_oap.inc()])
                po_free.freed_by((s_oap, s_oap.v))
                t_oev = (s_oap, s_oap.v)
                t_ors, (r_ap, m_ap) = emit_ln(
                    i, lambda lo, hi: ot_b[:, lo:hi],
                    (s_ost, s_osq, s_ors),
                    extra_wait_bn=[t_oev] + stats_free[i % 4].wait())
                p.op("scalar", lambda sE, r=r_ap, m=m_ap: sE.activation(
                    out=ot_b[:, :], in_=ot_b[:, :], func=AF.Identity, bias=m, scale=r),
                    wait=[t_ors], inc=[s_oap.inc()])
                stats_free[i % 4].freed_by((s_oap, s_oap.v))
                p.op("vector", lambda v, ib=ib: v.tensor_tensor(
                    out=x2_b[:, ib, :], in0=xr_b[:, ib, :], in1=ot_b[:, :], op=OP.add),
                    wait=[(s_oap, s_oap.v), t_xr] + x2b_free[ib].wait(), inc=[s_x2w.inc()])
                otb_free.freed_by((s_x2w, s_x2w.v))
                xr_free[ib].freed_by((s_x2w, s_x2w.v))
                p.op("gpsimd", lambda g, i=i, ib=ib: g.dma_start(
                    out=x2_d[i * 128:(i + 1) * 128, :], in_=x2_b[:, ib, :]),
                    wait=[(s_x2w, s_x2w.v)], inc=[s_x2s[ib].inc(16)])
                x2b_free[ib].freed_by((s_x2s[ib], s_x2s[ib].v))

        p.op("gpsimd", lambda g: g.nop(), wait=[(s_x2s[0], s_x2s[0].v), (s_x2s[1], s_x2s[1].v)])
        p.op("sync", lambda e: e.nop(),
             wait=[(s_x2ld[0], s_x2ld[0].v), (s_x2ld[1], s_x2ld[1].v),
                   (s_xr[0], s_xr[0].v), (s_xr[1], s_xr[1].v), (s_wld[0], s_wld[0].v), (s_wld[1], s_wld[1].v)])
        p.barrier("b2")
        es2.close()
        if PHASES < 3:
            for e in ENGINES:
                p.op(e, lambda eng: eng.nop(), wait=[(s_x2s[0], s_x2s[0].v), (s_x2s[1], s_x2s[1].v)])
            for e in ENGINES:
                p.op(e, lambda eng: eng.nop(), wait=[(s_xst[0], s_xst[0].v), (s_xst[1], s_xst[1].v)])
            for e in ENGINES:
                p.op(e, lambda eng: eng.nop(), wait=[(s_qst[0], s_qst[0].v), (s_qst[1], s_qst[1].v)])
            return _finish(nc, es_g, p)

        es3 = ExitStack()
        sb3 = lambda n, sh, dt=F32: es3.enter_context(nc.sbuf_tensor(n, sh, dt))
        ps3 = lambda n, sh, dt=F32: es3.enter_context(nc.psum_tensor(n, sh, dt))

        x2in = sb3("x2in", [128, 2, 2, D])
        xn2 = sb3("xn2", [128, D])
        xqf = sb3("xqf", [128, D], FP16)
        xqb = sb3("xqb", [128, D], BF16)
        xqT = sb3("xqT", [128, 2, 8, 256], BF16)
        w1qs = sb3("w1qs", [128, 2, 2, 8, 128], BF16)
        w2qs = sb3("w2qs", [128, 2, D], BF16)
        H0 = sb3("H0", [128, 32, 258])
        H1 = sb3("H1", [128, 32, 258])
        H2 = sb3("H2", [128, 32, 258])
        HS = (H0, H1, H2)
        tre = sb3("tre", [128, 4, 256])
        hqf = sb3("hqf", [128, 32, 256], FP16)
        hqb = sb3("hqb", [128, 32, 256], BF16)
        cmaxb = sb3("cmaxb", [128, 512])
        amaxc = sb3("amaxc", [128, 16])
        _amz = p.op("vector", lambda v: v.memset(amaxc[:], 0.0))
        sc2B = sb3("sc2B", [128, 256])
        invB = sb3("invB", [128, 2, 256])
        diagb = sb3("diagb", [128, 128])
        fac2c = sb3("fac2c", [128, 2])
        ffnb = sb3("ffnb", [128, D])
        pxq = ps3("pxq", [128, 1024], BF16)
        ph = [ps3("ph0", [128, 256]), ps3("ph1", [128, 256])]
        pbcst = ps3("pbcst", [128, 256])
        po2a = ps3("po2a", [128, 1024])
        po2b = ps3("po2b", [128, 1024])
        PO2 = (po2a, po2b)

        s_x3 = [p.sem("s_x30"), p.sem("s_x31")]
        s_st3 = p.sem("s_st3")
        s_sq3 = p.sem("s_sq3")
        s_rs3 = p.sem("s_rs3")
        s_xn3 = p.sem("s_xn3")
        s_qn3 = p.sem("s_qn3")
        s_xtr = p.sem("s_xtr")
        s_xte = p.sem("s_xte")
        s_w1l = [p.sem("s_w1l0"), p.sem("s_w1l1")]
        s_w2l = [p.sem("s_w2l0"), p.sem("s_w2l1")]
        s_hmm = p.sem("s_hmm")
        s_hev3 = p.sem("s_hev3")
        s_stash = p.sem("s_stash")
        s_conv = p.sem("s_conv")
        s_silu = p.sem("s_silu")
        s_tmax = p.sem("s_tmax")
        s_tmin = p.sem("s_tmin")
        s_pbp = p.sem("s_pbp")
        s_pbc = p.sem("s_pbc")
        s_amx = p.sem("s_amx")
        s_sc2 = p.sem("s_sc2")
        s_inv = p.sem("s_inv")
        s_rnd = p.sem("s_rnd")
        s_mid = p.sem("s_mid")
        s_w2m = p.sem("s_w2m")
        s_fev = p.sem("s_fev")
        s_fst = p.sem("s_fst")
        s_fsq = p.sem("s_fsq")
        s_frs = p.sem("s_frs")
        s_fap = p.sem("s_fap")
        s_ow = p.sem("s_ow")
        s_ost3 = p.sem("s_ost3")
        s_dbg3 = p.sem("s_dbg3")

        x2in_free = [Res(), Res()]
        am_dbg_tok = [None]
        xn2_free = Res()
        xq_free = Res()
        xqT_free = [Res(), Res()]
        w1qs_free = [Res(), Res()]
        w2qs_free = [Res(), Res()]
        ph_free = [Res(), Res()]
        slot_free = [Res(), Res(), Res()]
        tre_free = Res()
        hq_free = Res()
        cmax_free = Res()
        sc2_free = Res()
        inv_free = [Res(), Res()]
        diag_free = Res()
        ffnb_free = Res()
        stats_free = [Res(), Res(), Res(), Res()]
        fac2_free = Res()
        pb_uses = [0]

        def pb_produce(extra_wait=()):
            w = list(extra_wait)
            if pb_uses[0] > 0:
                w.append((s_pbc, pb_uses[0]))
            pb_uses[0] += 1
            return w

        def p3_lnquant(k, tl):
            i = 2 * k + tl
            buf = k % 2
            t_rs, (r_ap, m_ap) = emit_ln(
                i, lambda lo, hi, buf=buf, tl=tl: x2in[:, buf, tl, lo:hi],
                (s_st3, s_sq3, s_rs3),
                extra_wait_bn=[(s_x3[buf], 16 * (k // 2 + 1))] + stats_free[i % 4].wait())
            p.op("scalar", lambda sE, buf=buf, tl=tl, r=r_ap, m=m_ap: sE.activation(
                out=xn2[:, :], in_=x2in[:, buf, tl, :], func=AF.Identity, bias=m, scale=r),
                wait=[t_rs] + xn2_free.wait(), inc=[s_xn3.inc()])
            stats_free[i % 4].freed_by((s_xn3, s_xn3.v))
            t_xn = (s_xn3, s_xn3.v)
            p.op("vector", lambda v: v.reduce_max(
                out=amaxc[:, 0:1], in_=xn2[:, :], axis=AX.X, apply_absolute_value=True),
                wait=[t_xn], inc=[s_rg.inc()])
            p.op("vector", lambda v: v.tensor_scalar(
                out=amaxc[:, 1:2], in0=amaxc[:, 0:1], scalar1=1e-5, scalar2=None, op0=OP.max),
                wait=[(s_rg, s_rg.v)], inc=[s_rg.inc()])
            p.op("vector", lambda v: v.reciprocal(out=amaxc[:, 2:3], in_=amaxc[:, 1:2]),
                wait=[(s_rg, s_rg.v)], inc=[s_rg.inc()])
            p.op("vector", lambda v: v.tensor_scalar(
                out=amaxc[:, 3:4], in0=amaxc[:, 2:3], scalar1=127.0, scalar2=None, op0=OP.mult),
                wait=[(s_rg, s_rg.v)], inc=[s_rg.inc()])
            p.op("vector", lambda v, tl=tl: v.tensor_scalar(
                out=amaxc[:, 4 + tl:5 + tl], in0=amaxc[:, 1:2], scalar1=m127_1[:, 0:1],
                scalar2=None, op0=OP.mult), wait=[t_wsc, (s_rg, s_rg.v)], inc=[s_qn3.inc()])
            t_fac1 = (s_qn3, s_qn3.v)
            p.op("vector", lambda v: v.tensor_scalar(
                out=xqf[:, :], in0=xn2[:, :], scalar1=amaxc[:, 3:4], scalar2=1536.0,
                op0=OP.mult, op1=OP.add), wait=xq_free.wait(), inc=[s_qn3.inc()])
            xn2_free.freed_by((s_qn3, s_qn3.v))
            p.op("vector", lambda v: v.tensor_scalar(
                out=xqb[:, :], in0=xqf[:, :], scalar1=-1536.0, scalar2=None,
                op0=OP.add), inc=[s_qn3.inc()])
            t_xq = (s_qn3, s_qn3.v)
            if debug:
                p.op("gpsimd", lambda g, i=i: g.dma_start(out=xq_dbg[i], in_=xqb[:, :]),
                     wait=[t_xq, (s_dbg3, s_dbg3.v)], inc=[s_dbg3.inc(16)])
                xq_free.freed_by((s_dbg3, s_dbg3.v))
            p.op("vector", lambda v, tl=tl: v.tensor_scalar(
                out=diagb[:, :], in0=identf[:, :], scalar1=amaxc[:, 4 + tl:5 + tl],
                scalar2=None, op0=OP.mult),
                wait=[t_fac1] + diag_free.wait(), inc=[s_qn3.inc()])
            p.op("tensor", lambda t, tl=tl: t.matmul(
                out=pbcst[:, 0:128], lhsT=onesf[:, :], rhs=diagb[:, :],
                start=True, stop=True),
                wait=pb_produce([(s_qn3, s_qn3.v)]), inc=[s_pbp.inc()])
            diag_free.freed_by((s_pbp, s_pbp.v))
            p.op("vector", lambda v, buf=buf, tl=tl: v.tensor_copy(
                out=invB[:, buf, tl * 128:(tl + 1) * 128], in_=pbcst[:, 0:128]),
                wait=[(s_pbp, s_pbp.v)] + inv_free[buf].wait(), inc=[s_pbc.inc(), s_inv.inc()])
            def xtr(t):
                last = None
                for c in range(8):
                    last = t.transpose(out=pxq[:, c * 128:(c + 1) * 128],
                                       in_=xqb[:, c * 128:(c + 1) * 128], identity=identb[:])
                return last
            p.op("tensor", xtr, wait=[t_xq, t_ident] + [(s_xte, s_xte.v)], inc=[s_xtr.inc()])
            if debug:
                xq_free.freed_by((s_xtr, s_xtr.v), (s_dbg3, s_dbg3.v))
            else:
                xq_free.freed_by((s_xtr, s_xtr.v))
            p.op("vector", lambda v, buf=buf, tl=tl: v.tensor_copy(
                out=xqT[:, buf, :, tl * 128:(tl + 1) * 128],
                in_=pxq[:, :].rearrange("p (c t) -> p c t", c=8)),
                wait=[(s_xtr, s_xtr.v)] + (xqT_free[buf].wait() if tl == 0 else []),
                inc=[s_xte.inc()])

        def p3_mid(m):
            A = HS[m % 3]
            Y = HS[(m + 2) % 3]
            for s in range(32):
                p.op("vector", lambda v, s=s, A=A, Y=Y: v.tensor_scalar(
                    out=Y[:, s, 1:257], in0=A[:, s, 0:256], scalar1=w0col[:, s:s + 1],
                    scalar2=None, op0=OP.mult),
                    wait=([(s_stash, 2 * (m + 1))] + slot_free[(m + 2) % 3].wait() + [t_bias])
                    if s == 0 else [])
                p.op("vector", lambda v, s=s, A=A, Y=Y: v.scalar_tensor_tensor(
                    out=Y[:, s, 1:257], in0=A[:, s, 1:257], scalar=w1col[:, s:s + 1],
                    in1=Y[:, s, 1:257], op0=OP.mult, op1=OP.add))
                p.op("vector", lambda v, s=s, A=A, Y=Y: v.scalar_tensor_tensor(
                    out=Y[:, s, 1:257], in0=A[:, s, 2:258], scalar=w2col[:, s:s + 1],
                    in1=Y[:, s, 1:257], op0=OP.mult, op1=OP.add),
                    inc=[s_conv.inc()] if s == 31 else [])
            t_conv = (s_conv, s_conv.v)
            for s in range(32):
                p.op("scalar", lambda sE, s=s, Y=Y: sE.activation(
                    out=Y[:, s, 1:257], in_=Y[:, s, 1:257], func=AF.Silu,
                    bias=cbcol[:, s:s + 1], scale=1.0),
                    wait=[t_conv] if s == 0 else [],
                    inc=[s_silu.inc()] if s == 31 else [])
            t_silu = (s_silu, s_silu.v)
            p.op("vector", lambda v, Y=Y: v.tensor_tensor(
                out=tre[:, 0:4, :], in0=Y[:, 0:4, 1:257], in1=Y[:, 4:8, 1:257], op=OP.max),
                wait=[t_silu] + tre_free.wait())
            for blk in range(2, 8):
                p.op("vector", lambda v, Y=Y, blk=blk: v.tensor_tensor(
                    out=tre[:, 0:4, :], in0=tre[:, 0:4, :],
                    in1=Y[:, 4 * blk:4 * blk + 4, 1:257], op=OP.max))
            p.op("vector", lambda v: v.tensor_tensor(
                out=tre[:, 0:2, :], in0=tre[:, 0:2, :], in1=tre[:, 2:4, :], op=OP.max))
            p.op("vector", lambda v: v.tensor_tensor(
                out=cmaxb[:, 0:256], in0=tre[:, 0, :], in1=tre[:, 1, :], op=OP.max),
                wait=cmax_free.wait(), inc=[s_tmax.inc()])
            p.op("vector", lambda v, Y=Y: v.tensor_tensor(
                out=tre[:, 0:4, :], in0=Y[:, 0:4, 1:257], in1=Y[:, 4:8, 1:257], op=OP.min),
                wait=[(s_tmax, s_tmax.v)])
            for blk in range(2, 8):
                p.op("vector", lambda v, Y=Y, blk=blk: v.tensor_tensor(
                    out=tre[:, 0:4, :], in0=tre[:, 0:4, :],
                    in1=Y[:, 4 * blk:4 * blk + 4, 1:257], op=OP.min))
            p.op("vector", lambda v: v.tensor_tensor(
                out=tre[:, 0:2, :], in0=tre[:, 0:2, :], in1=tre[:, 2:4, :], op=OP.min))
            p.op("vector", lambda v: v.tensor_tensor(
                out=cmaxb[:, 256:512], in0=tre[:, 0, :], in1=tre[:, 1, :], op=OP.min),
                inc=[s_tmin.inc()])
            tre_free.freed_by((s_tmin, s_tmin.v))
            t_trees = [(s_tmax, s_tmax.v), (s_tmin, s_tmin.v)]
            for c in range(4):
                p.op("tensor", lambda t, c=c: t.transpose(
                    out=pbcst[:, 128:256], in_=cmaxb[:, c * 128:(c + 1) * 128],
                    identity=identf[:]),
                    wait=pb_produce((t_trees + ([am_dbg_tok[0]] if debug and am_dbg_tok[0] else []))
                                    if c == 0 else []), inc=[s_pbp.inc()])
                rop = OP.max if c < 2 else OP.min
                p.op("vector", lambda v, c=c, rop=rop: v.tensor_reduce(
                    out=amaxc[:, 8 + c:9 + c], in_=pbcst[:, 128:256], axis=AX.X, op=rop),
                    wait=[(s_pbp, s_pbp.v)], inc=[s_pbc.inc()] + ([s_rg.inc()] if c == 3 else []))
            cmax_free.freed_by((s_pbc, s_pbc.v))
            p.op("vector", lambda v: v.tensor_scalar(
                out=amaxc[:, 10:12], in0=amaxc[:, 10:12], scalar1=-1.0, scalar2=None,
                op0=OP.mult), wait=[(s_rg, s_rg.v)])
            p.op("vector", lambda v: v.tensor_tensor(
                out=amaxc[:, 8:10], in0=amaxc[:, 8:10], in1=amaxc[:, 10:12], op=OP.max))
            p.op("vector", lambda v: v.tensor_scalar(
                out=amaxc[:, 8:10], in0=amaxc[:, 8:10], scalar1=1e-5, scalar2=None, op0=OP.max),
                inc=[s_rg.inc()])
            p.op("vector", lambda v: v.reciprocal(out=amaxc[:, 10:12], in_=amaxc[:, 8:10]),
                wait=[(s_rg, s_rg.v)], inc=[s_rg.inc()])
            p.op("vector", lambda v: v.tensor_scalar(
                out=amaxc[:, 10:12], in0=amaxc[:, 10:12], scalar1=127.0, scalar2=None,
                op0=OP.mult), wait=[(s_rg, s_rg.v)])
            p.op("vector", lambda v: v.tensor_scalar(
                out=fac2c[:, 0:2], in0=amaxc[:, 8:10], scalar1=m127_2[:, 0:1], scalar2=None,
                op0=OP.mult), wait=[t_wsc] + fac2_free.wait(), inc=[s_amx.inc()])
            t_amx = (s_amx, s_amx.v)
            for tl in range(2):
                p.op("vector", lambda v, tl=tl: v.tensor_scalar(
                    out=diagb[:, :], in0=identf[:, :], scalar1=amaxc[:, 10 + tl:11 + tl],
                    scalar2=None, op0=OP.mult),
                    wait=[t_amx] + diag_free.wait(), inc=[s_sc2.inc()])
                p.op("tensor", lambda t, tl=tl: t.matmul(
                    out=pbcst[:, 0:128], lhsT=onesf[:, :], rhs=diagb[:, :], start=True, stop=True),
                    wait=pb_produce([(s_sc2, s_sc2.v)]), inc=[s_pbp.inc()])
                diag_free.freed_by((s_pbp, s_pbp.v))
                p.op("vector", lambda v, tl=tl: v.tensor_copy(
                    out=sc2B[:, tl * 128:(tl + 1) * 128], in_=pbcst[:, 0:128]),
                    wait=[(s_pbp, s_pbp.v)] + (sc2_free.wait() if tl == 0 else []),
                    inc=[s_pbc.inc(), s_sc2.inc()])
            t_sc2 = (s_sc2, s_sc2.v)
            p.op("gpsimd", lambda g, Y=Y: g.tensor_tensor(
                out=Y[:, :, 1:257], in0=Y[:, :, 1:257],
                in1=sc2B[:, None, :].broadcast_to([128, 32, 256]), op=OP.mult),
                wait=[(s_rnd, s_rnd.v), t_sc2, (s_pbc, pb_uses[0])], inc=[s_rnd.inc()])
            sc2_free.freed_by((s_rnd, s_rnd.v))
            slot_free[m % 3].freed_by((s_rnd, s_rnd.v))
            p.op("vector", lambda v, Y=Y: v.tensor_scalar(
                out=hqf[:, :, :], in0=Y[:, :, 1:257], scalar1=1536.0, scalar2=None,
                op0=OP.add),
                wait=[(s_rnd, s_rnd.v)] + hq_free.wait(), inc=[s_rnd.inc()])
            slot_free[(m + 2) % 3].freed_by((s_rnd, s_rnd.v))
            p.op("vector", lambda v: v.tensor_scalar(
                out=hqb[:, :, :], in0=hqf[:, :, :], scalar1=-1536.0, scalar2=None,
                op0=OP.add), inc=[s_rnd.inc(), s_mid.inc()])
            t_hq = (s_rnd, s_rnd.v)
            if debug:
                p.op("gpsimd", lambda g, m=m: g.dma_start(out=hq_dbg[m], in_=hqb[:, :, :]),
                     wait=[t_hq, (s_dbg3, s_dbg3.v)], inc=[s_dbg3.inc(16)])
                p.op("gpsimd", lambda g, m=m: g.dma_start(out=am_dbg[m], in_=amaxc[:, 8:16]),
                     wait=[t_hq, t_sc2, (s_dbg3, s_dbg3.v)], inc=[s_dbg3.inc(16)])
                am_dbg_tok[0] = (s_dbg3, s_dbg3.v)
                hq_free.freed_by((s_dbg3, s_dbg3.v))
            for s in range(32):
                wb = s % 2
                p.op("scalar", lambda e, s=s, wb=wb: e.dma_start(
                    out=w2qs[:, wb, :], in_=w2q_d[s]),
                    wait=w2qs_free[wb].wait() + ([t_wq_stored0, t_wq_stored1] if m == 0 and s < 2 else []),
                    inc=[s_w2l[wb].inc(16)])
                t_w2 = (s_w2l[wb], s_w2l[wb].v)

                def w2mm(t, s=s, wb=wb):
                    last = None
                    for tl in range(2):
                        for half in range(2):
                            last = t.matmul(
                                out=PO2[tl][:, half * 512:(half + 1) * 512],
                                lhsT=hqb[:, s, tl * 128:(tl + 1) * 128],
                                rhs=w2qs[:, wb, half * 512:(half + 1) * 512],
                                start=(s == 0), stop=(s == 31))
                    return last
                p.op("tensor", w2mm,
                     wait=[t_hq, t_w2] + (ph_free[0].wait() + ph_free[1].wait()
                                          if False else []) +
                     (PO2_free[0].wait() + PO2_free[1].wait() if s == 0 else []),
                     inc=[s_w2m.inc()])
                w2qs_free[wb].freed_by((s_w2m, s_w2m.v))
            if debug:
                hq_free.freed_by((s_w2m, s_w2m.v), (s_dbg3, s_dbg3.v))
            else:
                hq_free.freed_by((s_w2m, s_w2m.v))
            t_w2m = (s_w2m, s_w2m.v)
            for tl in range(2):
                i = 2 * m + tl
                mb = m % 2
                p.op("vector", lambda v, tl=tl: v.tensor_scalar(
                    out=ffnb[:, :], in0=PO2[tl][:, :], scalar1=fac2c[:, tl:tl + 1],
                    scalar2=None, op0=OP.mult),
                    wait=[t_w2m] + ffnb_free.wait(), inc=[s_fev.inc()])
                if tl == 1:
                    PO2_free[0].freed_by((s_fev, s_fev.v))
                    PO2_free[1].freed_by((s_fev, s_fev.v))
                    fac2_free.freed_by((s_fev, s_fev.v))
                t_frs, (r_ap, m_ap) = emit_ln(
                    i, lambda lo, hi: ffnb[:, lo:hi],
                    (s_fst, s_fsq, s_frs),
                    extra_wait_bn=[(s_fev, s_fev.v)] + stats_free[i % 4].wait())
                p.op("scalar", lambda sE, r=r_ap, m_=m_ap: sE.activation(
                    out=ffnb[:, :], in_=ffnb[:, :], func=AF.Identity, bias=m_, scale=r),
                    wait=[t_frs], inc=[s_fap.inc()])
                stats_free[i % 4].freed_by((s_fap, s_fap.v))
                p.op("vector", lambda v, tl=tl, mb=mb: v.tensor_tensor(
                    out=x2in[:, mb, tl, :], in0=x2in[:, mb, tl, :], in1=ffnb[:, :],
                    op=OP.add),
                    wait=[(s_fap, s_fap.v)], inc=[s_ow.inc()])
                ffnb_free.freed_by((s_ow, s_ow.v))
                p.op("gpsimd", lambda g, i=i, mb=mb, tl=tl: g.dma_start(
                    out=out_d[i * 128:(i + 1) * 128, :], in_=x2in[:, mb, tl, :]),
                    wait=[(s_ow, s_ow.v)], inc=[s_ost3.inc(16)])
                if tl == 1:
                    x2in_free[mb].freed_by((s_ost3, s_ost3.v))

        PO2_free = [Res(), Res()]

        for k in range(NCH):
            buf = k % 2
            slot = HS[k % 3]
            p.op("sync", lambda e, k=k, buf=buf: e.dma_start(
                out=x2in[:, buf, :, :],
                in_=x2_d[k * 256:(k + 1) * 256, :].rearrange("(t p) d -> p t d", p=128)),
                wait=x2in_free[buf].wait() + [(s_x2s[0], 16 * (k + 1)), (s_x2s[1], 16 * (k + 1))],
                inc=[s_x3[buf].inc(16)])
            w1toks = {}

            def emit_w1_load(b, k=k):
                wb = b % 2
                p.op("sync", lambda e, b=b, wb=wb: e.dma_start(
                    out=w1qs[:, wb, :, :, :], in_=w1q_d[2 * b:2 * (b + 1)].transpose([2, 0, 1, 3])),
                    wait=w1qs_free[wb].wait() + ([t_wq_stored0, t_wq_stored1] if k == 0 and b < 2 else []),
                    inc=[s_w1l[wb].inc(16)])
                w1toks[b] = (s_w1l[wb], s_w1l[wb].v)

            emit_w1_load(0)
            emit_w1_load(1)
            for tl in range(2):
                p3_lnquant(k, tl)
            t_xqT = (s_xte, s_xte.v)
            t_inv = (s_inv, s_inv.v)
            for s in range(32):
                b = s // 2
                pb = ph[s % 2]
                def w1mm(t, s=s, buf=buf, pb=pb):
                    last = None
                    for c in range(8):
                        last = t.matmul(out=pb[:, :], lhsT=w1qs[:, (s // 2) % 2, s % 2, c, :],
                                        rhs=xqT[:, buf, c, :], start=(c == 0), stop=(c == 7))
                    return last
                p.op("tensor", w1mm,
                     wait=[w1toks[b], t_xqT] + ph_free[s % 2].wait(), inc=[s_hmm.inc()])
                if s % 2 == 1:
                    w1qs_free[(s // 2) % 2].freed_by((s_hmm, s_hmm.v))
                    if b + 2 < 16:
                        emit_w1_load(b + 2)
                p.op("vector", lambda v, s=s, pb=pb, slot=slot, buf=buf: v.tensor_tensor(
                    out=slot[:, s, 1:257], in0=pb[:, :], in1=invB[:, buf, :], op=OP.mult),
                    wait=[(s_hmm, s_hmm.v), t_inv] +
                    (slot_free[k % 3].wait() if s == 0 else []),
                    inc=[s_hev3.inc()])
                ph_free[s % 2].freed_by((s_hev3, s_hev3.v))
            xqT_free[buf].freed_by((s_hmm, s_hmm.v))
            inv_free[buf].freed_by((s_hev3, s_hev3.v))
            t_hv = (s_hev3, s_hev3.v)
            if k >= 1:
                prev = HS[(k - 1) % 3]
                p.op("vector", lambda v, prev=prev, slot=slot: v.tensor_copy(
                    out=prev[:, :, 257:258], in_=slot[:, :, 1:2]),
                    wait=[t_hv], inc=[s_stash.inc()])
                p.op("vector", lambda v, prev=prev, slot=slot: v.tensor_copy(
                    out=slot[:, :, 0:1], in_=prev[:, :, 256:257]),
                    inc=[s_stash.inc()])
            else:
                p.op("vector", lambda v, slot=slot: v.memset(slot[:, :, 0:1], 0.0),
                     wait=[t_hv], inc=[s_stash.inc(2)])
            if k >= 1:
                p3_mid(k - 1)
        lastslot = HS[(NCH - 1) % 3]
        p.op("vector", lambda v: v.memset(lastslot[:, :, 257:258], 0.0),
             wait=[(s_hev3, s_hev3.v)], inc=[s_stash.inc(2)])
        p3_mid(NCH - 1)

        for e in ENGINES:
            p.op(e, lambda eng: eng.nop(), wait=[(s_ost3, 16 * NT)])
        for e in ENGINES:
            p.op(e, lambda eng: eng.nop(), wait=[(s_xst[0], s_xst[0].v), (s_xst[1], s_xst[1].v)])
        for e in ENGINES:
            p.op(e, lambda eng: eng.nop(), wait=[(s_x2s[0], s_x2s[0].v), (s_x2s[1], s_x2s[1].v)])
        for e in ENGINES:
            p.op(e, lambda eng: eng.nop(), wait=[(s_qst[0], s_qst[0].v), (s_qst[1], s_qst[1].v)])
        if debug:
            for e in ENGINES:
                p.op(e, lambda eng: eng.nop(), wait=[(s_dbg3, s_dbg3.v)])

        es3.close()
        return _finish(nc, es_g, p)


def _finish(nc, es_g, p):
    block = es_g.enter_context(nc.Block())

    @block.sync
    def _(eng):
        p.replay("sync", eng)

    @block.vector
    def _(eng):
        p.replay("vector", eng)

    @block.scalar
    def _(eng):
        p.replay("scalar", eng)

    @block.tensor
    def _(eng):
        p.replay("tensor", eng)

    @block.gpsimd
    def _(eng):
        p.replay("gpsimd", eng)
    return nc


_NC_CACHE = {}


def kernel(L0_attn_x, L0_attn_Wq, L0_attn_bq, L0_attn_Wk, L0_attn_bk, L0_attn_Wv,
           L0_attn_bv, L0_attn_Wg, L0_attn_bg, L0_attn_Wo, L0_attn_bo,
           L0_ffn_w1, L0_ffn_conv_w, L0_ffn_conv_b, L0_ffn_w2):
    if "nc" not in _NC_CACHE:
        _NC_CACHE["nc"] = build_nc(debug=False)
    nc = _NC_CACHE["nc"]
    x = np.ascontiguousarray(L0_attn_x, dtype=np.float32)
    shared = {
        "wq": np.ascontiguousarray(L0_attn_Wq, np.float32),
        "wk": np.ascontiguousarray(L0_attn_Wk, np.float32),
        "wv": np.ascontiguousarray(L0_attn_Wv, np.float32),
        "wg": np.ascontiguousarray(L0_attn_Wg, np.float32),
        "wo": np.ascontiguousarray(L0_attn_Wo, np.float32),
        "bq": np.ascontiguousarray(L0_attn_bq, np.float32),
        "bk": np.ascontiguousarray(L0_attn_bk, np.float32),
        "bv": np.ascontiguousarray(L0_attn_bv, np.float32),
        "bg": np.ascontiguousarray(L0_attn_bg, np.float32),
        "bo": np.ascontiguousarray(L0_attn_bo, np.float32),
        "w1": np.ascontiguousarray(L0_ffn_w1, np.float32),
        "cw": np.ascontiguousarray(L0_ffn_conv_w, np.float32).reshape(DH, 3),
        "cb": np.ascontiguousarray(L0_ffn_conv_b, np.float32),
        "w2": np.ascontiguousarray(L0_ffn_w2, np.float32),
    }
    in_maps = [dict(shared, x=np.ascontiguousarray(x[c])) for c in range(8)]
    res = run_bass_kernel_spmd(nc, in_maps, core_ids=list(range(8)))
    return np.stack([res.results[c]["out"] for c in range(8)], axis=0)


# version 04 (final)
# speedup vs baseline: 1.0253x; 1.0253x over previous
import os
import numpy as np
from contextlib import ExitStack

import concourse.bass as bass
import concourse.mybir as mybir
from concourse.bass_utils import run_bass_kernel_spmd

F32 = mybir.dt.float32
FP16 = mybir.dt.float16
BF16 = mybir.dt.bfloat16
I16 = mybir.dt.int16
I32 = mybir.dt.int32
OP = mybir.AluOpType
AF = mybir.ActivationFunctionType
AX = mybir.AxisListType

S = 4096
D = 1024
DH = 4096
NT = 32
NCH = 16
EPS = 1e-5

ENGINES = ("sync", "vector", "scalar", "tensor", "gpsimd")


class Sem:
    __slots__ = ("h", "v", "name")

    def __init__(self, nc, es, name):
        self.h = es.enter_context(nc.semaphore(name))
        self.v = 0
        self.name = name

    def inc(self, n=1):
        self.v += n
        return (self, n, self.v)

    def now(self):
        return (self, self.v)


class Res:
    __slots__ = ("free",)

    def __init__(self):
        self.free = ()

    def wait(self):
        return list(self.free)

    def freed_by(self, *toks):
        self.free = tuple(toks)


class Prog:
    CHAINED = ("vector",)

    def __init__(self, nc, es):
        self.nc = nc
        self.es = es
        self.ops = {e: [] for e in ENGINES}
        self.chain = {e: Sem(nc, es, f"chain_{e}") for e in self.CHAINED}
        self.last_tok = {e: None for e in self.CHAINED}

    def sem(self, name):
        return Sem(self.nc, self.es, name)

    def op(self, engine, fn, wait=(), inc=(), chain=True):
        w = []
        for t in wait:
            if t is None:
                continue
            s, v = t
            if v > 0:
                w.append((s, v))
        inc = list(inc)
        if chain and engine in self.chain:
            lt = self.last_tok[engine]
            if lt is not None:
                w.append(lt)
            if not inc:
                inc = [self.chain[engine].inc()]
            self.last_tok[engine] = (inc[0][0], inc[0][2])
        self.ops[engine].append((fn, tuple(w), tuple(inc)))

    def barrier(self, name):
        s = self.sem(name)
        for e in ENGINES:
            self.op(e, lambda eng: eng.nop(), inc=[s.inc(1)], chain=False)
        for e in ENGINES:
            self.op(e, lambda eng: eng.nop(), wait=[(s, len(ENGINES))], chain=False)

    def replay(self, engine, eng):
        for fn, waits, incs in self.ops[engine]:
            for (s, v) in waits:
                eng.wait_ge(s.h, v)
            ins = fn(eng)
            gate = None
            for k, (s, n, cum) in enumerate(incs):
                if k == 0:
                    ins.then_inc(s.h, n)
                    gate = (s, cum)
                else:
                    if gate is not None:
                        eng.wait_ge(gate[0].h, gate[1])
                        gate = None
                    eng.nop().then_inc(s.h, n)


def build_nc(debug=False):
    PHASES = int(os.environ.get("KPHASES", "3"))
    nc = bass.Bass("TRN2", target_bir_lowering=False)

    dI = lambda n, sh, dt=F32: nc.dram_tensor(n, sh, dt, kind="ExternalInput").ap()
    dO = lambda n, sh, dt=F32: nc.dram_tensor(n, sh, dt, kind="ExternalOutput").ap()
    dS = lambda n, sh, dt=F32: nc.dram_tensor(
        n, sh, dt, kind="ExternalOutput" if debug else "Internal").ap()

    x_in = dI("x", [S, D])
    wq_d, wk_d, wv_d, wg_d, wo_d = (dI(n, [D, D]) for n in ("wq", "wk", "wv", "wg", "wo"))
    bq_d, bk_d, bv_d, bg_d, bo_d = (dI(n, [D]) for n in ("bq", "bk", "bv", "bg", "bo"))
    w1_d = dI("w1", [DH, D])
    cw_d = dI("cw", [DH, 3])
    cb_d = dI("cb", [DH])
    w2_d = dI("w2", [D, DH])
    out_d = dO("out", [S, D])

    x2_d = dS("x2", [S, D])
    xnth_d = dS("xnth", [NT, 128, D], FP16)
    xntl_d = dS("xntl", [NT, 128, D], FP16)
    w1q_d = dS("w1q", [32, 8, 128, 128], BF16)
    w2q_d = dS("w2q", [32, 128, D], BF16)
    kv_dbg = dO("kv_dbg", [128, 1024]) if debug else None
    dbg_d = dO("dbg", [128, 64]) if debug else None
    xq_dbg = dO("xq_dbg", [NT, 128, D], BF16) if debug else None
    hq_dbg = dO("hq_dbg", [NCH, 128, 32, 256], BF16) if debug else None
    am_dbg = dO("am_dbg", [NCH, 128, 8], F32) if debug else None

    es_g = ExitStack()
    with es_g:
        sbg = lambda n, sh, dt=F32: es_g.enter_context(nc.sbuf_tensor(n, sh, dt))
        p = Prog(nc, es_g)

        s_rg = p.sem("s_rg")
        s_stq = p.sem("s_stq")
        identf = sbg("identf", [128, 128])
        identb = sbg("identb", [128, 128], BF16)
        ones1h = sbg("ones1h", [1, 128], FP16)
        onesf = sbg("onesf", [128, 128])
        bqcol = sbg("bqcol", [128, 8])
        bgcol = sbg("bgcol", [128, 8])
        bo_r = sbg("bo_r", [1, 2 * D], FP16)
        w0col = sbg("w0col", [128, 32])
        w1col = sbg("w1col", [128, 32])
        w2col = sbg("w2col", [128, 32])
        cbcol = sbg("cbcol", [128, 32])
        wsc1 = sbg("wsc1", [128, 1])
        wsc2 = sbg("wsc2", [128, 1])
        m127_1 = sbg("m127_1", [128, 1])
        m127_2 = sbg("m127_2", [128, 1])
        kv_h = sbg("kv_h", [128, 1024], FP16)
        kv_l = sbg("kv_l", [128, 1024], FP16)
        statsA = sbg("statsA", [128, 32])
        trashA = sbg("trashA", [128, 1024], BF16)
        ceps = sbg("ceps", [128, 1])
        dbgb = sbg("dbgb", [128, 64]) if debug else None
        nc.const_aps.register(F32, EPS, ceps.ap()[:, 0:1])

        def emit_ln(idx, src_fn, sems, extra_wait_bn=(), extra_wait_rs=()):
            s_st, s_sq, s_rs = sems
            base = 8 * (idx % 4)
            st = lambda a, b: statsA[:, base + a:base + b]
            p.op("vector", lambda v: v.reduce_sum(out=st(0, 1), in_=src_fn(0, 1024), axis=AX.X),
                 wait=list(extra_wait_bn), inc=[s_st.inc()])
            t_sta = (s_st, s_st.v)
            p.op("scalar", lambda sE: sE.activation(out=trashA[:, :], in_=src_fn(0, 1024),
                                                    func=AF.Square, accum_out=st(1, 2)),
                 wait=list(extra_wait_bn), inc=[s_stq.inc()])
            t_stb = (s_stq, s_stq.v)
            p.op("vector", lambda v: v.tensor_scalar(out=st(2, 3), in0=st(0, 1),
                                                     scalar1=1.0 / 1024.0, scalar2=None,
                                                     op0=OP.mult), wait=[t_sta, t_stb], inc=[s_rg.inc()])
            p.op("vector", lambda v: v.tensor_scalar(out=st(3, 4), in0=st(2, 3),
                                                     scalar1=st(2, 3), scalar2=None,
                                                     op0=OP.mult),
                 wait=[(s_rg, s_rg.v)], inc=[s_rg.inc()])
            p.op("vector", lambda v: v.scalar_tensor_tensor(out=st(4, 5), in0=st(1, 2),
                                                            scalar=1.0 / 1024.0, in1=st(3, 4),
                                                            op0=OP.mult, op1=OP.subtract),
                 wait=[(s_rg, s_rg.v)], inc=[s_sq.inc()])
            p.op("scalar", lambda sE: sE.activation(out=st(5, 6), in_=st(4, 5),
                                                    func=AF.Sqrt, bias=EPS, scale=1.0),
                 wait=[(s_sq, s_sq.v)], inc=[s_sq.inc()])
            p.op("vector", lambda v: v.reciprocal(out=st(6, 7), in_=st(5, 6)),
                 wait=[(s_sq, s_sq.v)] + list(extra_wait_rs), inc=[s_rg.inc()])
            p.op("vector", lambda v: v.scalar_tensor_tensor(out=st(7, 8), in0=st(2, 3),
                                                            scalar=-1.0, in1=st(6, 7),
                                                            op0=OP.mult, op1=OP.mult),
                 wait=[(s_rg, s_rg.v)], inc=[s_rs.inc()])
            return (s_rs, s_rs.v), (st(6, 7), st(7, 8))

        s_prep = p.sem("s_prep")
        p.op("gpsimd", lambda g: g.iota(identf[:], [[1, 128]], channel_multiplier=0,
                                        allow_small_or_imprecise_dtypes=True))
        p.op("gpsimd", lambda g: g.iota(onesf[:], [[0, 128]], channel_multiplier=1,
                                        allow_small_or_imprecise_dtypes=True),
             inc=[s_prep.inc()])
        p.op("vector", lambda v: v.tensor_tensor(out=identf[:], in0=identf[:], in1=onesf[:],
                                                 op=OP.is_equal), wait=[(s_prep, 1)])
        p.op("vector", lambda v: v.tensor_copy(out=identb[:], in_=identf[:]))
        p.op("vector", lambda v: v.memset(onesf[:], 1.0))
        p.op("vector", lambda v: v.memset(ceps[:], EPS))
        if debug:
            p.op("vector", lambda v: v.memset(dbgb[:], 0.0))
        p.op("vector", lambda v: v.memset(ones1h[:], 1.0), inc=[s_prep.inc()])
        t_ident = (s_prep, s_prep.v)

        s_bld = p.sem("s_bld")
        p.op("sync", lambda e: e.dma_start(out=bqcol[:], in_=bq_d.rearrange("(c p) -> p c", p=128), allow_slow_non_contiguous=True),
             inc=[s_bld.inc(16)])
        p.op("sync", lambda e: e.dma_start(out=bgcol[:], in_=bg_d.rearrange("(c p) -> p c", p=128), allow_slow_non_contiguous=True),
             inc=[s_bld.inc(16)])
        for k, col in ((0, w0col), (1, w1col), (2, w2col)):
            p.op("sync", lambda e, k=k, col=col: e.dma_start(
                out=col[:], in_=cw_d[:, k].rearrange("(s p) -> p s", p=128), allow_slow_non_contiguous=True), inc=[s_bld.inc(16)])
        p.op("sync", lambda e: e.dma_start(out=cbcol[:], in_=cb_d.rearrange("(s p) -> p s", p=128), allow_slow_non_contiguous=True),
             inc=[s_bld.inc(16)])

        es1 = ExitStack()
        sb1 = lambda n, sh, dt=F32: es1.enter_context(nc.sbuf_tensor(n, sh, dt))
        ps1 = lambda n, sh, dt=F32: es1.enter_context(nc.psum_tensor(n, sh, dt))

        bk_r = sb1("bk_r", [1, 2 * D], FP16)
        bv_r = sb1("bv_r", [1, 2 * D], FP16)
        brow_st = sb1("brow_st", [1, 3 * D])
        wk_h = sb1("wk_h", [128, 8, D], FP16)
        wk_l = sb1("wk_l", [128, 8, D], FP16)
        wv_h = sb1("wv_h", [128, 8, D], FP16)
        wv_l = sb1("wv_l", [128, 8, D], FP16)
        wstage = sb1("wstage", [128, 2, D])

        p.op("sync", lambda e: e.dma_start(out=brow_st[:, 0:D], in_=bk_d[None, :]), inc=[s_bld.inc(16)])
        p.op("sync", lambda e: e.dma_start(out=brow_st[:, D:2 * D], in_=bv_d[None, :]), inc=[s_bld.inc(16)])
        p.op("sync", lambda e: e.dma_start(out=brow_st[:, 2 * D:3 * D], in_=bo_d[None, :]), inc=[s_bld.inc(16)])
        brows_tok = (s_bld, s_bld.v)
        for (off, dst) in ((0, bk_r), (D, bv_r), (2 * D, bo_r)):
            p.op("vector", lambda v, o=off, d=dst: v.tensor_copy(out=d[:, 0:D], in_=brow_st[:, o:o + D]),
                 wait=[brows_tok])
            p.op("vector", lambda v, o=off, d=dst: v.scalar_tensor_tensor(
                out=d[:, D:2 * D], in0=d[:, 0:D], scalar=-1.0, in1=brow_st[:, o:o + D],
                op0=OP.mult, op1=OP.add))
        p.op("vector", lambda v: v.nop(), inc=[s_prep.inc()])
        t_bias = (s_prep, s_prep.v)

        s_wld = [p.sem("s_wld0"), p.sem("s_wld1")]
        s_wsp = p.sem("s_wsp")
        wstage_free = [Res(), Res()]

        def emit_w_split(wd, wh, wl):
            for c in range(8):
                buf = c % 2
                p.op("sync", lambda e, c=c, buf=buf, wd=wd: e.dma_start(
                    out=wstage[:, buf, :], in_=wd[c * 128:(c + 1) * 128, :]),
                    wait=wstage_free[buf].wait(), inc=[s_wld[buf].inc(16)])
                tok_ld = (s_wld[buf], s_wld[buf].v)
                p.op("vector", lambda v, c=c, buf=buf, wh=wh: v.tensor_copy(
                    out=wh[:, c, :], in_=wstage[:, buf, :]), wait=[tok_ld])
                p.op("vector", lambda v, c=c, buf=buf, wh=wh, wl=wl: v.scalar_tensor_tensor(
                    out=wl[:, c, :], in0=wh[:, c, :], scalar=-1.0, in1=wstage[:, buf, :],
                    op0=OP.mult, op1=OP.add), inc=[s_wsp.inc()])
                wstage_free[buf].freed_by((s_wsp, s_wsp.v))

        emit_w_split(wk_d, wk_h, wk_l)
        emit_w_split(wv_d, wv_h, wv_l)
        t_wkv = (s_wsp, s_wsp.v)

        es0 = ExitStack()
        sb0 = lambda n, sh, dt=F32: es0.enter_context(nc.sbuf_tensor(n, sh, dt))
        ps0 = lambda n, sh, dt=F32: es0.enter_context(nc.psum_tensor(n, sh, dt))
        qstage = sb0("qstage", [128, 2, DH])
        qs2 = sb0("qs2", [128, DH], BF16)
        qpos = sb0("qpos", [128, DH], BF16)
        wqout = sb0("wqout", [128, 2, 1024], BF16)
        wabs = sb0("wabs", [128, 44])
        svals = sb0("svals", [1, 8])
        pmean = ps0("pmean", [1, 2])
        pbc = ps0("pbc", [128, 6])

        s_qld = [p.sem("s_qld0"), p.sem("s_qld1")]
        s_qrd = p.sem("s_qrd")
        qstage_free = [Res(), Res()]
        for c in range(32):
            buf = c % 2
            p.op("sync", lambda e, c=c, buf=buf: e.dma_start(
                out=qstage[:, buf, 0:D], in_=w1_d[c * 128:(c + 1) * 128, :]),
                wait=qstage_free[buf].wait(), inc=[s_qld[buf].inc(16)])
            tok = (s_qld[buf], s_qld[buf].v)
            p.op("vector", lambda v, c=c, buf=buf: v.reduce_sum(
                out=wabs[:, c:c + 1], in_=qstage[:, buf, 0:D], axis=AX.X,
                apply_absolute_value=True), wait=[tok], inc=[s_qrd.inc()])
            qstage_free[buf].freed_by((s_qrd, s_qrd.v))
        for c in range(8):
            buf = c % 2
            p.op("sync", lambda e, c=c, buf=buf: e.dma_start(
                out=qstage[:, buf, :], in_=w2_d[c * 128:(c + 1) * 128, :]),
                wait=qstage_free[buf].wait(), inc=[s_qld[buf].inc(16)])
            tok = (s_qld[buf], s_qld[buf].v)
            p.op("vector", lambda v, c=c, buf=buf: v.reduce_sum(
                out=wabs[:, 32 + c:33 + c], in_=qstage[:, buf, :], axis=AX.X,
                apply_absolute_value=True), wait=[tok], inc=[s_qrd.inc()])
            qstage_free[buf].freed_by((s_qrd, s_qrd.v))
        p.op("vector", lambda v: v.reduce_sum(out=wabs[:, 40:41], in_=wabs[:, 0:32], axis=AX.X),
             wait=[(s_qrd, s_qrd.v)], inc=[s_rg.inc()])
        p.op("vector", lambda v: v.reduce_sum(out=wabs[:, 41:42], in_=wabs[:, 32:40], axis=AX.X),
             wait=[(s_rg, s_rg.v)], inc=[s_prep.inc()])
        t_wsum = (s_prep, s_prep.v)
        p.op("tensor", lambda t: t.matmul(out=pmean[0:1, 0:1], lhsT=wabs[:, 40:41],
                                          rhs=onesf[:, 0:1], start=True, stop=True),
             wait=[t_wsum, t_ident])
        p.op("tensor", lambda t: t.matmul(out=pmean[0:1, 1:2], lhsT=wabs[:, 41:42],
                                          rhs=onesf[:, 0:1], start=True, stop=True),
             inc=[s_prep.inc()])
        t_pm = (s_prep, s_prep.v)
        p.op("vector", lambda v: v.tensor_scalar(
            out=svals[0:1, 0:2], in0=pmean[0:1, 0:2], scalar1=1.0 / (DH * D), scalar2=1e-5,
            op0=OP.mult, op1=OP.max), wait=[t_pm], inc=[s_rg.inc()])
        p.op("vector", lambda v: v.reciprocal(out=svals[0:1, 2:4], in_=svals[0:1, 0:2]),
             wait=[(s_rg, s_rg.v)], inc=[s_rg.inc()])
        p.op("vector", lambda v: v.tensor_scalar(
            out=svals[0:1, 4:6], in0=svals[0:1, 0:2], scalar1=1.0 / 127.0, scalar2=None,
            op0=OP.mult), wait=[(s_rg, s_rg.v)], inc=[s_prep.inc()])
        t_sv = (s_prep, s_prep.v)
        for jj, cc in enumerate((2, 3, 4, 5)):
            p.op("tensor", lambda t, cc=cc: t.matmul(
                out=pbc[:, cc - 2:cc - 1], lhsT=onesf[0:1, :], rhs=svals[0:1, cc:cc + 1],
                start=True, stop=True),
                wait=[t_sv] if jj == 0 else [],
                inc=[s_prep.inc()] if jj == 3 else [])
        t_pbc = (s_prep, s_prep.v)
        p.op("vector", lambda v: v.tensor_copy(out=wsc1[:, :], in_=pbc[:, 0:1]), wait=[t_pbc])
        p.op("vector", lambda v: v.tensor_copy(out=wsc2[:, :], in_=pbc[:, 1:2]))
        p.op("vector", lambda v: v.tensor_copy(out=m127_1[:, :], in_=pbc[:, 2:3]))
        p.op("vector", lambda v: v.tensor_copy(out=m127_2[:, :], in_=pbc[:, 3:4]),
             inc=[s_prep.inc()])
        t_wsc = (s_prep, s_prep.v)
        if debug:
            pass
            p.op("vector", lambda v: v.tensor_copy(out=dbgb[:, 0:1], in_=wsc1[:, :]), wait=[t_wsc])
            p.op("vector", lambda v: v.tensor_copy(out=dbgb[:, 1:2], in_=m127_1[:, :]))
            p.op("vector", lambda v: v.tensor_copy(out=dbgb[:, 2:3], in_=wsc2[:, :]))
            p.op("vector", lambda v: v.tensor_copy(out=dbgb[:, 3:4], in_=m127_2[:, :]))
            p.op("vector", lambda v: v.tensor_copy(out=dbgb[0:1, 8:14], in_=svals[0:1, 0:6]))
            p.op("vector", lambda v: v.tensor_copy(out=dbgb[0:1, 14:16], in_=pmean[0:1, 0:2]))
            p.op("vector", lambda v: v.tensor_copy(out=dbgb[:, 16:18], in_=wabs[:, 40:42]),
                 inc=[s_prep.inc()])
            t_dbg0 = (s_prep, s_prep.v)

        ptq = ps0("ptq", [128, 1024], BF16)
        s_q2 = p.sem("s_q2")
        s_qmm = p.sem("s_qmm")
        s_qev = p.sem("s_qev")
        s_qst = [p.sem("s_qst0"), p.sem("s_qst1")]
        qst_n = [0]
        qs2_free = Res()
        ptq_free = Res()
        wqout_free = [Res(), Res()]

        def emit_quant(src_ap, n_chunks, width, wsc, which):
            for c in range(n_chunks):
                buf = c % 2
                p.op("sync", lambda e, c=c, buf=buf, src=src_ap, w=width: e.dma_start(
                    out=qstage[:, buf, 0:w], in_=src[c * 128:(c + 1) * 128, :]),
                    wait=qstage_free[buf].wait(), inc=[s_qld[buf].inc(16)])
                tok_ld = (s_qld[buf], s_qld[buf].v)
                p.op("vector", lambda v, buf=buf, w=width, wsc=wsc: v.tensor_scalar(
                    out=qpos[:, 0:w], in0=qstage[:, buf, 0:w], scalar1=wsc[:, 0:1],
                    scalar2=0.5, op0=OP.mult, op1=OP.is_gt),
                    wait=[tok_ld, t_wsc] + qs2_free.wait())
                p.op("vector", lambda v, buf=buf, w=width, wsc=wsc: v.tensor_scalar(
                    out=qs2[:, 0:w], in0=qstage[:, buf, 0:w], scalar1=wsc[:, 0:1],
                    scalar2=-0.5, op0=OP.mult, op1=OP.is_lt), inc=[s_qrd.inc()])
                qstage_free[buf].freed_by((s_qrd, s_qrd.v))
                p.op("vector", lambda v, w=width: v.tensor_tensor(
                    out=qs2[:, 0:w], in0=qpos[:, 0:w], in1=qs2[:, 0:w], op=OP.subtract),
                    inc=[s_q2.inc()])
                t_q2 = (s_q2, s_q2.v)
                ngrp = width // 1024
                for g in range(ngrp):
                    for tt in range(8):
                        p.op("tensor", lambda t, g=g, tt=tt: t.transpose(
                            out=ptq[:, tt * 128:(tt + 1) * 128],
                            in_=qs2[:, g * 1024 + tt * 128:g * 1024 + (tt + 1) * 128],
                            identity=identb[:]),
                            wait=([t_q2, t_ident] + ptq_free.wait()) if tt == 0 else [],
                            inc=[s_qmm.inc()] if tt == 7 else [])
                    t_tr = (s_qmm, s_qmm.v)
                    qb = qst_n[0] % 2
                    qst_n[0] += 1
                    p.op("scalar", lambda sE, qb=qb: sE.activation(
                        out=wqout[:, qb, :], in_=ptq[:, :], func=AF.Identity),
                        wait=[t_tr] + wqout_free[qb].wait(), inc=[s_qev.inc()])
                    ptq_free.freed_by((s_qev, s_qev.v))
                    t_ev = (s_qev, s_qev.v)
                    if which == "w1":
                        p.op("gpsimd", lambda gE, c=c, qb=qb: gE.dma_start(
                            out=w1q_d[c].transpose([1, 0, 2]),
                            in_=wqout[:, qb, :].rearrange("p (t c) -> p t c", t=8)),
                            wait=[t_ev], inc=[s_qst[qb].inc(16)])
                    else:
                        p.op("gpsimd", lambda gE, c=c, g=g, qb=qb: gE.dma_start(
                            out=w2q_d[8 * g:8 * (g + 1), :, c * 128:(c + 1) * 128].transpose([1, 0, 2]),
                            in_=wqout[:, qb, :].rearrange("p (t c) -> p t c", t=8)),
                            wait=[t_ev], inc=[s_qst[qb].inc(16)])
                    wqout_free[qb].freed_by((s_qst[qb], s_qst[qb].v))
                qs2_free.freed_by((s_qmm, s_qmm.v))

        emit_quant(w1_d, 32, D, wsc1, "w1")
        emit_quant(w2_d, 8, DH, wsc2, "w2")
        t_wq_stored0 = (s_qst[0], s_qst[0].v)
        t_wq_stored1 = (s_qst[1], s_qst[1].v)

        p.op("sync", lambda e: e.nop(), wait=[(s_qld[0], s_qld[0].v), (s_qld[1], s_qld[1].v)])
        p.op("gpsimd", lambda g: g.nop(), wait=[(s_qst[0], s_qst[0].v), (s_qst[1], s_qst[1].v)])
        p.op("sync", lambda e: e.nop(), wait=[(s_wld[0], s_wld[0].v), (s_wld[1], s_wld[1].v), (s_bld, s_bld.v)])
        p.barrier("b0")
        es0.close()

        xin = sb1("xin", [128, 2, D])
        xnb = sb1("xnb", [128, 2, D])
        kvacc = sb1("kvacc", [128, 1024])
        ktmp = sb1("ktmp", [128, D])
        kf32 = sb1("kf32", [128, D])
        k_h = sb1("k_h", [128, D], FP16)
        k_l = sb1("k_l", [128, D], FP16)
        v_h = sb1("v_h", [128, D], FP16)
        v_l = sb1("v_l", [128, D], FP16)
        xnth_b = sb1("xnth_b", [128, 2, D], FP16)
        xntl_b = sb1("xntl_b", [128, 2, D], FP16)
        ptr = ps1("ptr", [128, 1024])
        pk = ps1("pk", [128, 1024])
        pv = ps1("pv", [128, 1024])
        pkv = ps1("pkv", [128, 1024])

        s_x = [p.sem("s_x0"), p.sem("s_x1")]
        s_st1 = p.sem("s_st1")
        s_sq1 = p.sem("s_sq1")
        s_rs1 = p.sem("s_rs1")
        s_xn = p.sem("s_xn")
        s_tr = p.sem("s_tr")
        s_hev = p.sem("s_hev")
        s_lev = p.sem("s_lev")
        s_xst = [p.sem("s_xst0"), p.sem("s_xst1")]
        s_kvp = p.sem("s_kvp")
        s_kch = p.sem("s_kch")
        s_vch = p.sem("s_vch")
        s_kvm = p.sem("s_kvm")
        s_kva = p.sem("s_kva")

        xin_free = [Res(), Res()]
        xnb_free = [Res(), Res()]
        xT_free = [Res(), Res()]
        ptr_free = Res()
        pk_free = Res()
        pv_free = Res()
        pkv_free = Res()
        ktmp_free = Res()
        stats_free = [Res(), Res(), Res(), Res()]

        for i in range(NT):
            buf = i % 2
            p.op("sync", lambda e, i=i, buf=buf: e.dma_start(
                out=xin[:, buf, :], in_=x_in[i * 128:(i + 1) * 128, :]),
                wait=xin_free[buf].wait(), inc=[s_x[buf].inc(16)])
            t_x = (s_x[buf], s_x[buf].v)
            t_rs, (rstd_ap, nmr_ap) = emit_ln(
                i, lambda lo, hi, buf=buf: xin[:, buf, lo:hi],
                (s_st1, s_sq1, s_rs1),
                extra_wait_bn=[t_x] + stats_free[i % 4].wait())
            p.op("scalar", lambda sE, buf=buf, r=rstd_ap, m=nmr_ap: sE.activation(
                out=xnb[:, buf, :], in_=xin[:, buf, :], func=AF.Identity,
                bias=m, scale=r),
                wait=[t_rs] + xnb_free[buf].wait(), inc=[s_xn.inc()])
            xin_free[buf].freed_by((s_xn, s_xn.v))
            t_xn = (s_xn, s_xn.v)
            for c in range(8):
                p.op("tensor", lambda t, c=c, buf=buf: t.transpose(
                    out=ptr[:, c * 128:(c + 1) * 128], in_=xnb[:, buf, c * 128:(c + 1) * 128],
                    identity=identf[:]),
                    wait=([t_xn, t_ident] + ptr_free.wait()) if c == 0 else [],
                    inc=[s_tr.inc()] if c == 7 else [])
            xnb_free[buf].freed_by((s_tr, s_tr.v))
            t_tr = (s_tr, s_tr.v)
            p.op("scalar", lambda sE, buf=buf: sE.activation(
                out=xnth_b[:, buf, :], in_=ptr[:, :], func=AF.Identity),
                wait=[t_tr] + xT_free[buf].wait(), inc=[s_hev.inc()])
            t_h = (s_hev, s_hev.v)
            p.op("vector", lambda v, buf=buf: v.scalar_tensor_tensor(
                out=xntl_b[:, buf, :], in0=xnth_b[:, buf, :], scalar=-1.0, in1=ptr[:, :],
                op0=OP.mult, op1=OP.add),
                wait=[t_tr, t_h] + xT_free[buf].wait(), inc=[s_lev.inc()])
            t_l = (s_lev, s_lev.v)
            ptr_free.freed_by(t_h, t_l)
            stats_free[i % 4].freed_by(t_xn)
            p.op("gpsimd", lambda g, i=i, buf=buf: g.dma_start(
                out=xnth_d[i], in_=xnth_b[:, buf, :]), wait=[t_h], inc=[s_xst[buf].inc(16)])
            p.op("gpsimd", lambda g, i=i, buf=buf: g.dma_start(
                out=xntl_d[i], in_=xntl_b[:, buf, :]), wait=[t_l], inc=[s_xst[buf].inc(16)])
            t_xst = (s_xst[buf], s_xst[buf].v)

            def kv_proj(t, buf=buf):
                last = None
                for (pp, wh, wl, br) in ((pk, wk_h, wk_l, bk_r), (pv, wv_h, wv_l, bv_r)):
                    for half in range(2):
                        o = pp[:, half * 512:(half + 1) * 512]
                        first = True
                        for (xs, ws) in ((xnth_b, wh), (xnth_b, wl), (xntl_b, wh)):
                            for c in range(8):
                                last = t.matmul(
                                    out=o, lhsT=xs[:, buf, c * 128:(c + 1) * 128],
                                    rhs=ws[:, c, half * 512:(half + 1) * 512],
                                    start=first, stop=False)
                                first = False
                        last = t.matmul(out=o, lhsT=ones1h[:, :],
                                        rhs=br[0:1, half * 512:half * 512 + 512],
                                        start=False, stop=False)
                        last = t.matmul(out=o, lhsT=ones1h[:, :],
                                        rhs=br[0:1, D + half * 512:D + half * 512 + 512],
                                        start=False, stop=True)
                return last
            p.op("tensor", kv_proj,
                 wait=[t_h, t_l, t_wkv, t_bias] + pk_free.wait() + pv_free.wait(),
                 inc=[s_kvp.inc()])
            xT_free[buf].freed_by((s_kvp, s_kvp.v), t_xst)
            t_kvp = (s_kvp, s_kvp.v)
            p.op("vector", lambda v: v.tensor_scalar(out=ktmp[:, :], in0=pk[:, :], scalar1=0.0,
                                                     scalar2=None, op0=OP.min),
                 wait=[t_kvp] + ktmp_free.wait(), inc=[s_kch.inc()])
            p.op("scalar", lambda sE: sE.activation(out=ktmp[:, :], in_=ktmp[:, :], func=AF.Exp),
                 wait=[(s_kch, s_kch.v)], inc=[s_kch.inc()])
            p.op("vector", lambda v: v.scalar_tensor_tensor(out=kf32[:, :], in0=pk[:, :],
                                                            scalar=0.0, in1=ktmp[:, :],
                                                            op0=OP.max, op1=OP.add),
                 wait=[(s_kch, s_kch.v)], inc=[s_kch.inc()])
            pk_free.freed_by((s_kch, s_kch.v))
            ktmp_free.freed_by((s_kch, s_kch.v))
            p.op("vector", lambda v: v.tensor_copy(out=k_h[:, :], in_=kf32[:, :]))
            p.op("vector", lambda v: v.scalar_tensor_tensor(out=k_l[:, :], in0=k_h[:, :],
                                                            scalar=-1.0, in1=kf32[:, :],
                                                            op0=OP.mult, op1=OP.add),
                 inc=[s_kch.inc()])
            t_k = (s_kch, s_kch.v)
            p.op("scalar", lambda sE: sE.activation(out=v_h[:, :], in_=pv[:, :], func=AF.Identity),
                 wait=[t_kvp] + ([] if i == 0 else []), inc=[s_vch.inc()])
            p.op("vector", lambda v: v.scalar_tensor_tensor(out=v_l[:, :], in0=v_h[:, :],
                                                            scalar=-1.0, in1=pv[:, :],
                                                            op0=OP.mult, op1=OP.add),
                 wait=[(s_vch, s_vch.v)], inc=[s_vch.inc()])
            pv_free.freed_by((s_vch, s_vch.v))
            t_v = (s_vch, s_vch.v)

            def kv_mm(t):
                last = None
                for pr in range(8):
                    sl = slice(pr * 128, (pr + 1) * 128)
                    last = t.matmul(out=pkv[:, sl], lhsT=k_h[:, sl], rhs=v_h[:, sl],
                                    start=True, stop=False)
                    last = t.matmul(out=pkv[:, sl], lhsT=k_h[:, sl], rhs=v_l[:, sl],
                                    start=False, stop=False)
                    last = t.matmul(out=pkv[:, sl], lhsT=k_l[:, sl], rhs=v_h[:, sl],
                                    start=False, stop=True)
                return last
            p.op("tensor", kv_mm, wait=[t_k, t_v] + pkv_free.wait(), inc=[s_kvm.inc()])
            t_kvmm = (s_kvm, s_kvm.v)
            if i == 0:
                p.op("vector", lambda v: v.tensor_copy(out=kvacc[:, :], in_=pkv[:, :]),
                     wait=[t_kvmm], inc=[s_kva.inc()])
            else:
                p.op("vector", lambda v: v.scalar_tensor_tensor(
                    out=kvacc[:, :], in0=pkv[:, :], scalar=1.0, in1=kvacc[:, :],
                    op0=OP.mult, op1=OP.add), wait=[t_kvmm], inc=[s_kva.inc()])
            pkv_free.freed_by((s_kva, s_kva.v))
            ktmp_free.freed_by((s_kva, s_kva.v))

        if debug:
            p.op("vector", lambda v: v.tensor_copy(out=dbgb[:, 32:64], in_=statsA[:, 0:32]),
                 wait=[(s_rs1, 1)], inc=[s_prep.inc()])
            p.op("gpsimd", lambda g: g.dma_start(out=dbg_d, in_=dbgb[:, :]),
                 wait=[(s_kva, s_kva.v), (s_prep, s_prep.v)], inc=[s_qst[0].inc(16)])
        p.op("vector", lambda v: v.tensor_copy(out=kv_h[:, :], in_=kvacc[:, :]),
             wait=[(s_kva, s_kva.v)])
        p.op("vector", lambda v: v.scalar_tensor_tensor(out=kv_l[:, :], in0=kv_h[:, :],
                                                        scalar=-1.0, in1=kvacc[:, :],
                                                        op0=OP.mult, op1=OP.add),
             inc=[s_prep.inc()])
        t_kv = (s_prep, s_prep.v)
        if debug:
            p.op("gpsimd", lambda g: g.dma_start(out=kv_dbg, in_=kvacc[:, :]),
                 wait=[t_kv], inc=[s_qst[0].inc(16)])
            p.op("gpsimd", lambda g: g.nop(), wait=[(s_qst[0], s_qst[0].v)])

        p.op("gpsimd", lambda g: g.nop(),
             wait=[(s_xst[0], s_xst[0].v), (s_xst[1], s_xst[1].v),
                   (s_qst[0], s_qst[0].v), (s_qst[1], s_qst[1].v)])
        p.op("sync", lambda e: e.nop(), wait=[(s_x[0], s_x[0].v), (s_x[1], s_x[1].v)])
        p.barrier("b1")
        es1.close()
        if PHASES < 2:
            for e in ENGINES:
                p.op(e, lambda eng: eng.nop(),
                     wait=[(s_xst[0], s_xst[0].v), (s_xst[1], s_xst[1].v)])
            for e in ENGINES:
                p.op(e, lambda eng: eng.nop(),
                     wait=[(s_qst[0], s_qst[0].v), (s_qst[1], s_qst[1].v)])
            return _finish(nc, es_g, p)

        es2 = ExitStack()
        sb2 = lambda n, sh, dt=F32: es2.enter_context(nc.sbuf_tensor(n, sh, dt))
        ps2 = lambda n, sh, dt=F32: es2.enter_context(nc.psum_tensor(n, sh, dt))

        wq_h = sb2("wq_h", [128, 8, D], FP16)
        wq_l = sb2("wq_l", [128, 8, D], FP16)
        wg_h = sb2("wg_h", [128, 8, D], FP16)
        wg_l = sb2("wg_l", [128, 8, D], FP16)
        wo_h = sb2("wo_h", [128, 8, D], FP16)
        wo_l = sb2("wo_l", [128, 8, D], FP16)
        wstage2 = sb2("wstage2", [128, 2, D])

        wstage_free = [Res(), Res()]

        def emit_w_split2(wd, wh, wl):
            for c in range(8):
                buf = c % 2
                p.op("sync", lambda e, c=c, buf=buf, wd=wd: e.dma_start(
                    out=wstage2[:, buf, :], in_=wd[c * 128:(c + 1) * 128, :]),
                    wait=wstage_free[buf].wait(), inc=[s_wld[buf].inc(16)])
                tok_ld = (s_wld[buf], s_wld[buf].v)
                p.op("vector", lambda v, c=c, buf=buf, wh=wh: v.tensor_copy(
                    out=wh[:, c, :], in_=wstage2[:, buf, :]), wait=[tok_ld])
                p.op("vector", lambda v, c=c, buf=buf, wh=wh, wl=wl: v.scalar_tensor_tensor(
                    out=wl[:, c, :], in0=wh[:, c, :], scalar=-1.0, in1=wstage2[:, buf, :],
                    op0=OP.mult, op1=OP.add), inc=[s_wsp.inc()])
                wstage_free[buf].freed_by((s_wsp, s_wsp.v))

        if os.environ.get("KEXIT") == "preW":
            for e in ENGINES:
                p.op(e, lambda eng: eng.nop(), wait=[(s_xst[0], s_xst[0].v), (s_xst[1], s_xst[1].v)])
            for e in ENGINES:
                p.op(e, lambda eng: eng.nop(), wait=[(s_qst[0], s_qst[0].v), (s_qst[1], s_qst[1].v)])
            es2.close()
            return _finish(nc, es_g, p)
        if int(os.environ.get("KP2W", "1")):
            emit_w_split2(wq_d, wq_h, wq_l)
            emit_w_split2(wg_d, wg_h, wg_l)
            emit_w_split2(wo_d, wo_h, wo_l)
        t_wqgo = (s_wsp, s_wsp.v)

        xh_b = sb2("xh_b", [128, 2, 2, D], FP16)
        xl_b = sb2("xl_b", [128, 2, 2, D], FP16)
        qf32 = sb2("qf32", [128, 8, 256])
        qtmp = sb2("qtmp", [128, 256])
        q_hb = sb2("q_hb", [128, 8, 256], FP16)
        q_lb = sb2("q_lb", [128, 8, 256], FP16)
        gT = sb2("gT", [128, 8, 256])
        gav_h = sb2("gav_h", [128, 8, 256], FP16)
        gav_l = sb2("gav_l", [128, 8, 256], FP16)
        xr_b = sb2("xr_b", [128, 2, D])
        ot_b = sb2("ot_b", [128, D])
        x2_b = sb2("x2_b", [128, 2, D])
        pq2 = [ps2("pq0", [128, 256]), ps2("pq1", [128, 256])]
        pg2 = [ps2("pg0", [128, 256]), ps2("pg1", [128, 256])]
        pa2 = [ps2("pa0", [128, 256]), ps2("pa1", [128, 256])]
        po = ps2("po", [128, 1024])

        s_x2ld = [p.sem("s_x2ld0"), p.sem("s_x2ld1")]
        s_qmm2 = p.sem("s_qmm2")
        s_qch = p.sem("s_qch")
        s_gmm = p.sem("s_gmm")
        s_gch = p.sem("s_gch")
        s_amm = p.sem("s_amm")
        s_gav = p.sem("s_gav")
        s_omm = p.sem("s_omm")
        s_ost = p.sem("s_ost")
        s_osq = p.sem("s_osq")
        s_ors = p.sem("s_ors")
        s_oap = p.sem("s_oap")
        s_xr = [p.sem("s_xr0"), p.sem("s_xr1")]
        s_x2w = p.sem("s_x2w")
        s_x2s = [p.sem("s_x2s0"), p.sem("s_x2s1")]

        xTb_free = [Res(), Res()]
        pq_free = [Res(), Res()]
        pg_free = [Res(), Res()]
        pa_free = [Res(), Res()]
        po_free = Res()
        qtmp_free = Res()
        qf_free = [Res() for _ in range(8)]
        qhl_free = [Res() for _ in range(8)]
        gT_free = [Res() for _ in range(8)]
        gav_free = [Res() for _ in range(8)]
        xr_free = [Res(), Res()]
        otb_free = Res()
        x2b_free = [Res(), Res()]
        stats_free = [Res(), Res(), Res(), Res()]

        KP2N = int(os.environ.get("KP2N", str(NCH)))
        KP2SUB = int(os.environ.get("KP2SUB", "5"))
        for j in range(KP2N):
            buf = j % 2
            p.op("sync", lambda e, j=j, buf=buf: e.dma_start(
                out=xh_b[:, buf, :, :], in_=xnth_d[2 * j:2 * j + 2].transpose([1, 0, 2])),
                wait=xTb_free[buf].wait() + [(s_xst[0], 32 * NT // 2), (s_xst[1], 32 * NT // 2)], inc=[s_x2ld[buf].inc(16)])
            p.op("sync", lambda e, j=j, buf=buf: e.dma_start(
                out=xl_b[:, buf, :, :], in_=xntl_d[2 * j:2 * j + 2].transpose([1, 0, 2])),
                inc=[s_x2ld[buf].inc(16)])
            t_xT = (s_x2ld[buf], s_x2ld[buf].v)
            for fo in range(8):
                pq = pq2[fo % 2]
                def q_mm(t, fo=fo, buf=buf, pq=pq):
                    last = None
                    passes = ((xh_b, wq_h), (xl_b, wq_h), (xh_b, wq_l))
                    for pi, (xs, ws) in enumerate(passes):
                        for c in range(8):
                            last = t.matmul(
                                out=pq[:, :], lhsT=ws[:, c, fo * 128:(fo + 1) * 128],
                                rhs=xs[:, buf, :, c * 128:(c + 1) * 128],
                                start=(pi == 0 and c == 0), stop=(pi == 2 and c == 7))
                    return last
                p.op("tensor", q_mm, wait=[t_xT, t_wqgo] + pq_free[fo % 2].wait(), inc=[s_qmm2.inc()])
                t_qmm = (s_qmm2, s_qmm2.v)
                p.op("vector", lambda v, fo=fo, pq=pq: v.tensor_scalar(
                    out=qtmp[:, :], in0=pq[:, :], scalar1=bqcol[:, fo:fo + 1], scalar2=0.0,
                    op0=OP.add, op1=OP.min),
                    wait=[t_qmm] + qtmp_free.wait(), inc=[s_qch.inc()])
                p.op("scalar", lambda sE: sE.activation(out=qtmp[:, :], in_=qtmp[:, :], func=AF.Exp),
                     wait=[(s_qch, s_qch.v)], inc=[s_qch.inc()])
                p.op("vector", lambda v, fo=fo, pq=pq: v.tensor_scalar(
                    out=qf32[:, fo, :], in0=pq[:, :], scalar1=bqcol[:, fo:fo + 1], scalar2=0.0,
                    op0=OP.add, op1=OP.max),
                    wait=[(s_qch, s_qch.v)] + qf_free[fo].wait(), inc=[s_qch.inc()])
                pq_free[fo % 2].freed_by((s_qch, s_qch.v))
                p.op("vector", lambda v, fo=fo: v.tensor_tensor(
                    out=qf32[:, fo, :], in0=qf32[:, fo, :], in1=qtmp[:, :], op=OP.add),
                    inc=[s_qch.inc()])
                qtmp_free.freed_by((s_qch, s_qch.v))
                p.op("vector", lambda v, fo=fo: v.tensor_copy(out=q_hb[:, fo, :], in_=qf32[:, fo, :]),
                     wait=qhl_free[fo].wait())
                p.op("vector", lambda v, fo=fo: v.scalar_tensor_tensor(
                    out=q_lb[:, fo, :], in0=q_hb[:, fo, :], scalar=-1.0, in1=qf32[:, fo, :],
                    op0=OP.mult, op1=OP.add), inc=[s_qch.inc()])
                qf_free[fo].freed_by((s_qch, s_qch.v))
            t_q = (s_qch, s_qch.v)
            for fo in range(8 if KP2SUB >= 2 else 0):
                pg = pg2[fo % 2]
                def g_mm(t, fo=fo, buf=buf, pg=pg):
                    last = None
                    passes = ((xh_b, wg_h), (xl_b, wg_h), (xh_b, wg_l))
                    for pi, (xs, ws) in enumerate(passes):
                        for c in range(8):
                            last = t.matmul(
                                out=pg[:, :], lhsT=ws[:, c, fo * 128:(fo + 1) * 128],
                                rhs=xs[:, buf, :, c * 128:(c + 1) * 128],
                                start=(pi == 0 and c == 0), stop=(pi == 2 and c == 7))
                    return last
                p.op("tensor", g_mm, wait=[t_xT, t_wqgo] + pg_free[fo % 2].wait(), inc=[s_gmm.inc()])
                t_gmm = (s_gmm, s_gmm.v)
                p.op("scalar", lambda sE, fo=fo, pg=pg: sE.activation(
                    out=gT[:, fo, :], in_=pg[:, :], func=AF.Sigmoid, bias=bgcol[:, fo:fo + 1],
                    scale=1.0),
                    wait=[t_gmm] + gT_free[fo].wait(), inc=[s_gch.inc()])
                pg_free[fo % 2].freed_by((s_gch, s_gch.v))
            xTb_free[buf].freed_by((s_qmm2, s_qmm2.v), (s_gmm, s_gmm.v))
            for pr in range(8 if KP2SUB >= 3 else 0):
                pa = pa2[pr % 2]
                def a_mm(t, pr=pr, pa=pa):
                    c0 = pr * 128
                    t.matmul(out=pa[0:64, :], lhsT=kv_h[0:64, c0:c0 + 64],
                             rhs=q_hb[0:64, pr, :], start=True, stop=False)
                    t.matmul(out=pa[0:64, :], lhsT=kv_h[0:64, c0:c0 + 64],
                             rhs=q_lb[0:64, pr, :], start=False, stop=False)
                    t.matmul(out=pa[0:64, :], lhsT=kv_l[0:64, c0:c0 + 64],
                             rhs=q_hb[0:64, pr, :], start=False, stop=True)
                    t.matmul(out=pa[64:128, :], lhsT=kv_h[64:128, c0 + 64:c0 + 128],
                             rhs=q_hb[64:128, pr, :], start=True, stop=False,
                             tile_position=(64, 64))
                    t.matmul(out=pa[64:128, :], lhsT=kv_h[64:128, c0 + 64:c0 + 128],
                             rhs=q_lb[64:128, pr, :], start=False, stop=False,
                             tile_position=(64, 64))
                    return t.matmul(out=pa[64:128, :], lhsT=kv_l[64:128, c0 + 64:c0 + 128],
                                    rhs=q_hb[64:128, pr, :], start=False, stop=True,
                                    tile_position=(64, 64))
                p.op("tensor", a_mm, wait=[t_q, t_kv] + pa_free[pr % 2].wait(), inc=[s_amm.inc()])
                t_amm = (s_amm, s_amm.v)
                p.op("vector", lambda v, pr=pr, pa=pa: v.scalar_tensor_tensor(
                    out=qf32[:, pr, :], in0=pa[:, :], scalar=1.0, in1=gT[:, pr, :],
                    op0=OP.mult, op1=OP.mult),
                    wait=[t_amm, (s_gch, s_gch.v)] + qf_free[pr].wait(), inc=[s_gav.inc()])
                pa_free[pr % 2].freed_by((s_gav, s_gav.v))
                gT_free[pr].freed_by((s_gav, s_gav.v))
                qhl_free[pr].freed_by((s_amm, s_amm.v))
                p.op("vector", lambda v, pr=pr: v.tensor_copy(
                    out=gav_h[:, pr, :], in_=qf32[:, pr, :]),
                    wait=gav_free[pr].wait())
                p.op("vector", lambda v, pr=pr: v.scalar_tensor_tensor(
                    out=gav_l[:, pr, :], in0=gav_h[:, pr, :], scalar=-1.0, in1=qf32[:, pr, :],
                    op0=OP.mult, op1=OP.add), inc=[s_gav.inc()])
                qf_free[pr].freed_by((s_gav, s_gav.v))
            t_gav = (s_gav, s_gav.v)
            for tl in range(2 if KP2SUB >= 4 else 0):
                i = 2 * j + tl
                ib = i % 2
                p.op("sync", lambda e, i=i, ib=ib: e.dma_start(
                    out=xr_b[:, ib, :], in_=x_in[i * 128:(i + 1) * 128, :]),
                    wait=xr_free[ib].wait(), inc=[s_xr[ib].inc(16)])
                t_xr = (s_xr[ib], s_xr[ib].v)

                def o_mm(t, tl=tl):
                    last = None
                    for half in range(2):
                        o = po[:, half * 512:(half + 1) * 512]
                        first = True
                        for (gs, ws) in ((gav_h, wo_h), (gav_h, wo_l), (gav_l, wo_h)):
                            for c in range(8):
                                last = t.matmul(
                                    out=o, lhsT=gs[:, c, tl * 128:(tl + 1) * 128],
                                    rhs=ws[:, c, half * 512:(half + 1) * 512],
                                    start=first, stop=False)
                                first = False
                        last = t.matmul(out=o, lhsT=ones1h[:, :],
                                        rhs=bo_r[0:1, half * 512:half * 512 + 512],
                                        start=False, stop=False)
                        last = t.matmul(out=o, lhsT=ones1h[:, :],
                                        rhs=bo_r[0:1, D + half * 512:D + half * 512 + 512],
                                        start=False, stop=True)
                    return last
                p.op("tensor", o_mm, wait=[t_gav, t_bias] + po_free.wait(), inc=[s_omm.inc()])
                t_omm = (s_omm, s_omm.v)
                if tl == 1:
                    for pr in range(8):
                        gav_free[pr].freed_by(t_omm)
                p.op("scalar", lambda sE: sE.activation(
                    out=ot_b[:, :], in_=po[:, :], func=AF.Identity),
                    wait=[t_omm] + otb_free.wait(), inc=[s_oap.inc()])
                po_free.freed_by((s_oap, s_oap.v))
                t_oev = (s_oap, s_oap.v)
                t_ors, (r_ap, m_ap) = emit_ln(
                    i, lambda lo, hi: ot_b[:, lo:hi],
                    (s_ost, s_osq, s_ors),
                    extra_wait_bn=[t_oev] + stats_free[i % 4].wait())
                p.op("scalar", lambda sE, r=r_ap, m=m_ap: sE.activation(
                    out=ot_b[:, :], in_=ot_b[:, :], func=AF.Identity, bias=m, scale=r),
                    wait=[t_ors], inc=[s_oap.inc()])
                stats_free[i % 4].freed_by((s_oap, s_oap.v))
                p.op("vector", lambda v, ib=ib: v.tensor_tensor(
                    out=x2_b[:, ib, :], in0=xr_b[:, ib, :], in1=ot_b[:, :], op=OP.add),
                    wait=[(s_oap, s_oap.v), t_xr] + x2b_free[ib].wait(), inc=[s_x2w.inc()])
                otb_free.freed_by((s_x2w, s_x2w.v))
                xr_free[ib].freed_by((s_x2w, s_x2w.v))
                p.op("gpsimd", lambda g, i=i, ib=ib: g.dma_start(
                    out=x2_d[i * 128:(i + 1) * 128, :], in_=x2_b[:, ib, :]),
                    wait=[(s_x2w, s_x2w.v)], inc=[s_x2s[ib].inc(16)])
                x2b_free[ib].freed_by((s_x2s[ib], s_x2s[ib].v))

        p.op("gpsimd", lambda g: g.nop(), wait=[(s_x2s[0], s_x2s[0].v), (s_x2s[1], s_x2s[1].v)])
        p.op("sync", lambda e: e.nop(),
             wait=[(s_x2ld[0], s_x2ld[0].v), (s_x2ld[1], s_x2ld[1].v),
                   (s_xr[0], s_xr[0].v), (s_xr[1], s_xr[1].v), (s_wld[0], s_wld[0].v), (s_wld[1], s_wld[1].v)])
        p.barrier("b2")
        es2.close()
        if PHASES < 3:
            for e in ENGINES:
                p.op(e, lambda eng: eng.nop(), wait=[(s_x2s[0], s_x2s[0].v), (s_x2s[1], s_x2s[1].v)])
            for e in ENGINES:
                p.op(e, lambda eng: eng.nop(), wait=[(s_xst[0], s_xst[0].v), (s_xst[1], s_xst[1].v)])
            for e in ENGINES:
                p.op(e, lambda eng: eng.nop(), wait=[(s_qst[0], s_qst[0].v), (s_qst[1], s_qst[1].v)])
            return _finish(nc, es_g, p)

        es3 = ExitStack()
        sb3 = lambda n, sh, dt=F32: es3.enter_context(nc.sbuf_tensor(n, sh, dt))
        ps3 = lambda n, sh, dt=F32: es3.enter_context(nc.psum_tensor(n, sh, dt))

        x2in = sb3("x2in", [128, 2, 2, D])
        xn2 = sb3("xn2", [128, D])
        xqf = sb3("xqf", [128, D], FP16)
        xqb = sb3("xqb", [128, D], BF16)
        xqT = sb3("xqT", [128, 2, 8, 256], BF16)
        w1qs = sb3("w1qs", [128, 2, 4, 8, 128], BF16)
        w2qs = sb3("w2qs", [128, 2, D], BF16)
        H0 = sb3("H0", [128, 32, 258])
        H1 = sb3("H1", [128, 32, 258])
        H2 = sb3("H2", [128, 32, 258])
        HS = (H0, H1, H2)
        tre = sb3("tre", [128, 4, 256])
        hqf = sb3("hqf", [128, 32, 256], FP16)
        hqb = sb3("hqb", [128, 32, 256], BF16)
        cmaxb = sb3("cmaxb", [128, 512])
        amaxc = sb3("amaxc", [128, 16])
        _amz = p.op("vector", lambda v: v.memset(amaxc[:], 0.0))
        sc2B = sb3("sc2B", [128, 256])
        invB = sb3("invB", [128, 2, 256])
        diagb = sb3("diagb", [128, 128])
        fac2c = sb3("fac2c", [128, 2])
        ffnb = sb3("ffnb", [128, D])
        pxq = ps3("pxq", [128, 1024], BF16)
        ph = [ps3("ph0", [128, 256]), ps3("ph1", [128, 256])]
        pbcst = ps3("pbcst", [128, 256])
        po2a = ps3("po2a", [128, 1024])
        po2b = ps3("po2b", [128, 1024])
        PO2 = (po2a, po2b)

        s_x3 = [p.sem("s_x30"), p.sem("s_x31")]
        s_st3 = p.sem("s_st3")
        s_sq3 = p.sem("s_sq3")
        s_rs3 = p.sem("s_rs3")
        s_xn3 = p.sem("s_xn3")
        s_qn3 = p.sem("s_qn3")
        s_xtr = p.sem("s_xtr")
        s_xte = p.sem("s_xte")
        s_w1l = [p.sem("s_w1l0"), p.sem("s_w1l1")]
        s_w2l = [p.sem("s_w2l0"), p.sem("s_w2l1")]
        s_hmm = p.sem("s_hmm")
        s_hev3 = p.sem("s_hev3")
        s_stash = p.sem("s_stash")
        s_conv = p.sem("s_conv")
        s_silu = p.sem("s_silu")
        s_tmax = p.sem("s_tmax")
        s_tmin = p.sem("s_tmin")
        s_pbp = p.sem("s_pbp")
        s_pbc = p.sem("s_pbc")
        s_amx = p.sem("s_amx")
        s_sc2 = p.sem("s_sc2")
        s_inv = p.sem("s_inv")
        s_rnd = p.sem("s_rnd")
        s_mid = p.sem("s_mid")
        s_w2m = p.sem("s_w2m")
        s_fev = p.sem("s_fev")
        s_fst = p.sem("s_fst")
        s_fsq = p.sem("s_fsq")
        s_frs = p.sem("s_frs")
        s_fap = p.sem("s_fap")
        s_ow = p.sem("s_ow")
        s_ost3 = p.sem("s_ost3")
        s_dbg3 = p.sem("s_dbg3")

        x2in_free = [Res(), Res()]
        am_dbg_tok = [None]
        xn2_free = Res()
        xq_free = Res()
        xqT_free = [Res(), Res()]
        w1qs_free = [Res(), Res()]
        w2qs_free = [Res(), Res()]
        ph_free = [Res(), Res()]
        slot_free = [Res(), Res(), Res()]
        tre_free = Res()
        hq_free = Res()
        cmax_free = Res()
        sc2_free = Res()
        inv_free = [Res(), Res()]
        diag_free = Res()
        ffnb_free = Res()
        stats_free = [Res(), Res(), Res(), Res()]
        fac2_free = Res()
        pb_uses = [0]

        def pb_produce(extra_wait=()):
            w = list(extra_wait)
            if pb_uses[0] > 0:
                w.append((s_pbc, pb_uses[0]))
            pb_uses[0] += 1
            return w

        def p3_lnquant(k, tl):
            i = 2 * k + tl
            buf = k % 2
            t_rs, (r_ap, m_ap) = emit_ln(
                i, lambda lo, hi, buf=buf, tl=tl: x2in[:, buf, tl, lo:hi],
                (s_st3, s_sq3, s_rs3),
                extra_wait_bn=[(s_x3[buf], 16 * (k // 2 + 1))] + stats_free[i % 4].wait())
            p.op("scalar", lambda sE, buf=buf, tl=tl, r=r_ap, m=m_ap: sE.activation(
                out=xn2[:, :], in_=x2in[:, buf, tl, :], func=AF.Identity, bias=m, scale=r),
                wait=[t_rs] + xn2_free.wait(), inc=[s_xn3.inc()])
            stats_free[i % 4].freed_by((s_xn3, s_xn3.v))
            t_xn = (s_xn3, s_xn3.v)
            p.op("vector", lambda v: v.reduce_max(
                out=amaxc[:, 0:1], in_=xn2[:, :], axis=AX.X, apply_absolute_value=True),
                wait=[t_xn], inc=[s_rg.inc()])
            p.op("vector", lambda v: v.tensor_scalar(
                out=amaxc[:, 1:2], in0=amaxc[:, 0:1], scalar1=1e-5, scalar2=None, op0=OP.max),
                wait=[(s_rg, s_rg.v)], inc=[s_rg.inc()])
            p.op("vector", lambda v: v.reciprocal(out=amaxc[:, 2:3], in_=amaxc[:, 1:2]),
                wait=[(s_rg, s_rg.v)], inc=[s_rg.inc()])
            p.op("vector", lambda v: v.tensor_scalar(
                out=amaxc[:, 3:4], in0=amaxc[:, 2:3], scalar1=127.0, scalar2=None, op0=OP.mult),
                wait=[(s_rg, s_rg.v)], inc=[s_rg.inc()])
            p.op("vector", lambda v, tl=tl: v.tensor_scalar(
                out=amaxc[:, 4 + tl:5 + tl], in0=amaxc[:, 1:2], scalar1=m127_1[:, 0:1],
                scalar2=None, op0=OP.mult), wait=[t_wsc, (s_rg, s_rg.v)], inc=[s_qn3.inc()])
            t_fac1 = (s_qn3, s_qn3.v)
            p.op("vector", lambda v: v.tensor_scalar(
                out=xqf[:, :], in0=xn2[:, :], scalar1=amaxc[:, 3:4], scalar2=1536.0,
                op0=OP.mult, op1=OP.add), wait=xq_free.wait(), inc=[s_qn3.inc()])
            xn2_free.freed_by((s_qn3, s_qn3.v))
            p.op("vector", lambda v: v.tensor_scalar(
                out=xqb[:, :], in0=xqf[:, :], scalar1=-1536.0, scalar2=None,
                op0=OP.add), inc=[s_qn3.inc()])
            t_xq = (s_qn3, s_qn3.v)
            if debug:
                p.op("gpsimd", lambda g, i=i: g.dma_start(out=xq_dbg[i], in_=xqb[:, :]),
                     wait=[t_xq, (s_dbg3, s_dbg3.v)], inc=[s_dbg3.inc(16)])
                xq_free.freed_by((s_dbg3, s_dbg3.v))
            p.op("vector", lambda v, tl=tl: v.tensor_scalar(
                out=diagb[:, :], in0=identf[:, :], scalar1=amaxc[:, 4 + tl:5 + tl],
                scalar2=None, op0=OP.mult),
                wait=[t_fac1] + diag_free.wait(), inc=[s_qn3.inc()])
            p.op("tensor", lambda t, tl=tl: t.matmul(
                out=pbcst[:, 0:128], lhsT=onesf[:, :], rhs=diagb[:, :],
                start=True, stop=True),
                wait=pb_produce([(s_qn3, s_qn3.v)]), inc=[s_pbp.inc()])
            diag_free.freed_by((s_pbp, s_pbp.v))
            p.op("vector", lambda v, buf=buf, tl=tl: v.tensor_copy(
                out=invB[:, buf, tl * 128:(tl + 1) * 128], in_=pbcst[:, 0:128]),
                wait=[(s_pbp, s_pbp.v)] + inv_free[buf].wait(), inc=[s_pbc.inc(), s_inv.inc()])
            def xtr(t):
                last = None
                for c in range(8):
                    last = t.transpose(out=pxq[:, c * 128:(c + 1) * 128],
                                       in_=xqb[:, c * 128:(c + 1) * 128], identity=identb[:])
                return last
            p.op("tensor", xtr, wait=[t_xq, t_ident] + [(s_xte, s_xte.v)], inc=[s_xtr.inc()])
            if debug:
                xq_free.freed_by((s_xtr, s_xtr.v), (s_dbg3, s_dbg3.v))
            else:
                xq_free.freed_by((s_xtr, s_xtr.v))
            p.op("vector", lambda v, buf=buf, tl=tl: v.tensor_copy(
                out=xqT[:, buf, :, tl * 128:(tl + 1) * 128],
                in_=pxq[:, :].rearrange("p (c t) -> p c t", c=8)),
                wait=[(s_xtr, s_xtr.v)] + (xqT_free[buf].wait() if tl == 0 else []),
                inc=[s_xte.inc()])

        def p3_mid(m):
            A = HS[m % 3]
            Y = HS[(m + 2) % 3]
            for s in range(32):
                p.op("vector", lambda v, s=s, A=A, Y=Y: v.tensor_scalar(
                    out=Y[:, s, 1:257], in0=A[:, s, 0:256], scalar1=w0col[:, s:s + 1],
                    scalar2=None, op0=OP.mult),
                    wait=([(s_stash, 2 * (m + 1))] + slot_free[(m + 2) % 3].wait() + [t_bias])
                    if s == 0 else [])
                p.op("vector", lambda v, s=s, A=A, Y=Y: v.scalar_tensor_tensor(
                    out=Y[:, s, 1:257], in0=A[:, s, 1:257], scalar=w1col[:, s:s + 1],
                    in1=Y[:, s, 1:257], op0=OP.mult, op1=OP.add))
                p.op("vector", lambda v, s=s, A=A, Y=Y: v.scalar_tensor_tensor(
                    out=Y[:, s, 1:257], in0=A[:, s, 2:258], scalar=w2col[:, s:s + 1],
                    in1=Y[:, s, 1:257], op0=OP.mult, op1=OP.add),
                    inc=[s_conv.inc()] if s == 31 else [])
            t_conv = (s_conv, s_conv.v)
            for s in range(32):
                p.op("scalar", lambda sE, s=s, Y=Y: sE.activation(
                    out=Y[:, s, 1:257], in_=Y[:, s, 1:257], func=AF.Silu,
                    bias=cbcol[:, s:s + 1], scale=1.0),
                    wait=[t_conv] if s == 0 else [],
                    inc=[s_silu.inc()] if s == 31 else [])
            t_silu = (s_silu, s_silu.v)
            p.op("vector", lambda v, Y=Y: v.tensor_tensor(
                out=tre[:, 0:4, :], in0=Y[:, 0:4, 1:257], in1=Y[:, 4:8, 1:257], op=OP.max),
                wait=[t_silu] + tre_free.wait())
            for blk in range(2, 8):
                p.op("vector", lambda v, Y=Y, blk=blk: v.tensor_tensor(
                    out=tre[:, 0:4, :], in0=tre[:, 0:4, :],
                    in1=Y[:, 4 * blk:4 * blk + 4, 1:257], op=OP.max))
            p.op("vector", lambda v: v.tensor_tensor(
                out=tre[:, 0:2, :], in0=tre[:, 0:2, :], in1=tre[:, 2:4, :], op=OP.max))
            p.op("vector", lambda v: v.tensor_tensor(
                out=cmaxb[:, 0:256], in0=tre[:, 0, :], in1=tre[:, 1, :], op=OP.max),
                wait=cmax_free.wait(), inc=[s_tmax.inc()])
            p.op("vector", lambda v, Y=Y: v.tensor_tensor(
                out=tre[:, 0:4, :], in0=Y[:, 0:4, 1:257], in1=Y[:, 4:8, 1:257], op=OP.min),
                wait=[(s_tmax, s_tmax.v)])
            for blk in range(2, 8):
                p.op("vector", lambda v, Y=Y, blk=blk: v.tensor_tensor(
                    out=tre[:, 0:4, :], in0=tre[:, 0:4, :],
                    in1=Y[:, 4 * blk:4 * blk + 4, 1:257], op=OP.min))
            p.op("vector", lambda v: v.tensor_tensor(
                out=tre[:, 0:2, :], in0=tre[:, 0:2, :], in1=tre[:, 2:4, :], op=OP.min))
            p.op("vector", lambda v: v.tensor_tensor(
                out=cmaxb[:, 256:512], in0=tre[:, 0, :], in1=tre[:, 1, :], op=OP.min),
                inc=[s_tmin.inc()])
            tre_free.freed_by((s_tmin, s_tmin.v))
            t_trees = [(s_tmax, s_tmax.v), (s_tmin, s_tmin.v)]
            for c in range(4):
                p.op("tensor", lambda t, c=c: t.transpose(
                    out=pbcst[:, 128:256], in_=cmaxb[:, c * 128:(c + 1) * 128],
                    identity=identf[:]),
                    wait=pb_produce((t_trees + ([am_dbg_tok[0]] if debug and am_dbg_tok[0] else []))
                                    if c == 0 else []), inc=[s_pbp.inc()])
                rop = OP.max if c < 2 else OP.min
                p.op("vector", lambda v, c=c, rop=rop: v.tensor_reduce(
                    out=amaxc[:, 8 + c:9 + c], in_=pbcst[:, 128:256], axis=AX.X, op=rop),
                    wait=[(s_pbp, s_pbp.v)], inc=[s_pbc.inc()] + ([s_rg.inc()] if c == 3 else []))
            cmax_free.freed_by((s_pbc, s_pbc.v))
            p.op("vector", lambda v: v.tensor_scalar(
                out=amaxc[:, 10:12], in0=amaxc[:, 10:12], scalar1=-1.0, scalar2=None,
                op0=OP.mult), wait=[(s_rg, s_rg.v)])
            p.op("vector", lambda v: v.tensor_tensor(
                out=amaxc[:, 8:10], in0=amaxc[:, 8:10], in1=amaxc[:, 10:12], op=OP.max))
            p.op("vector", lambda v: v.tensor_scalar(
                out=amaxc[:, 8:10], in0=amaxc[:, 8:10], scalar1=1e-5, scalar2=None, op0=OP.max),
                inc=[s_rg.inc()])
            p.op("vector", lambda v: v.reciprocal(out=amaxc[:, 10:12], in_=amaxc[:, 8:10]),
                wait=[(s_rg, s_rg.v)], inc=[s_rg.inc()])
            p.op("vector", lambda v: v.tensor_scalar(
                out=amaxc[:, 10:12], in0=amaxc[:, 10:12], scalar1=127.0, scalar2=None,
                op0=OP.mult), wait=[(s_rg, s_rg.v)])
            p.op("vector", lambda v: v.tensor_scalar(
                out=fac2c[:, 0:2], in0=amaxc[:, 8:10], scalar1=m127_2[:, 0:1], scalar2=None,
                op0=OP.mult), wait=[t_wsc] + fac2_free.wait(), inc=[s_amx.inc()])
            t_amx = (s_amx, s_amx.v)
            for tl in range(2):
                p.op("vector", lambda v, tl=tl: v.tensor_scalar(
                    out=diagb[:, :], in0=identf[:, :], scalar1=amaxc[:, 10 + tl:11 + tl],
                    scalar2=None, op0=OP.mult),
                    wait=[t_amx] + diag_free.wait(), inc=[s_sc2.inc()])
                p.op("tensor", lambda t, tl=tl: t.matmul(
                    out=pbcst[:, 0:128], lhsT=onesf[:, :], rhs=diagb[:, :], start=True, stop=True),
                    wait=pb_produce([(s_sc2, s_sc2.v)]), inc=[s_pbp.inc()])
                diag_free.freed_by((s_pbp, s_pbp.v))
                p.op("vector", lambda v, tl=tl: v.tensor_copy(
                    out=sc2B[:, tl * 128:(tl + 1) * 128], in_=pbcst[:, 0:128]),
                    wait=[(s_pbp, s_pbp.v)] + (sc2_free.wait() if tl == 0 else []),
                    inc=[s_pbc.inc(), s_sc2.inc()])
            t_sc2 = (s_sc2, s_sc2.v)
            p.op("gpsimd", lambda g, Y=Y: g.tensor_tensor(
                out=Y[:, :, 1:257], in0=Y[:, :, 1:257],
                in1=sc2B[:, None, :].broadcast_to([128, 32, 256]), op=OP.mult),
                wait=[(s_rnd, s_rnd.v), t_sc2, (s_pbc, pb_uses[0])], inc=[s_rnd.inc()])
            sc2_free.freed_by((s_rnd, s_rnd.v))
            slot_free[m % 3].freed_by((s_rnd, s_rnd.v))
            p.op("vector", lambda v, Y=Y: v.tensor_scalar(
                out=hqf[:, :, :], in0=Y[:, :, 1:257], scalar1=1536.0, scalar2=None,
                op0=OP.add),
                wait=[(s_rnd, s_rnd.v)] + hq_free.wait(), inc=[s_rnd.inc()])
            slot_free[(m + 2) % 3].freed_by((s_rnd, s_rnd.v))
            p.op("vector", lambda v: v.tensor_scalar(
                out=hqb[:, :, :], in0=hqf[:, :, :], scalar1=-1536.0, scalar2=None,
                op0=OP.add), inc=[s_rnd.inc(), s_mid.inc()])
            t_hq = (s_rnd, s_rnd.v)
            if debug:
                p.op("gpsimd", lambda g, m=m: g.dma_start(out=hq_dbg[m], in_=hqb[:, :, :]),
                     wait=[t_hq, (s_dbg3, s_dbg3.v)], inc=[s_dbg3.inc(16)])
                p.op("gpsimd", lambda g, m=m: g.dma_start(out=am_dbg[m], in_=amaxc[:, 8:16]),
                     wait=[t_hq, t_sc2, (s_dbg3, s_dbg3.v)], inc=[s_dbg3.inc(16)])
                am_dbg_tok[0] = (s_dbg3, s_dbg3.v)
                hq_free.freed_by((s_dbg3, s_dbg3.v))
            for s in range(32):
                wb = s % 2
                p.op("scalar", lambda e, s=s, wb=wb: e.dma_start(
                    out=w2qs[:, wb, :], in_=w2q_d[s]),
                    wait=w2qs_free[wb].wait() + ([t_wq_stored0, t_wq_stored1] if m == 0 and s < 2 else []),
                    inc=[s_w2l[wb].inc(16)])
                t_w2 = (s_w2l[wb], s_w2l[wb].v)

                def w2mm(t, s=s, wb=wb):
                    last = None
                    for tl in range(2):
                        for half in range(2):
                            last = t.matmul(
                                out=PO2[tl][:, half * 512:(half + 1) * 512],
                                lhsT=hqb[:, s, tl * 128:(tl + 1) * 128],
                                rhs=w2qs[:, wb, half * 512:(half + 1) * 512],
                                start=(s == 0), stop=(s == 31))
                    return last
                p.op("tensor", w2mm,
                     wait=[t_hq, t_w2] + (ph_free[0].wait() + ph_free[1].wait()
                                          if False else []) +
                     (PO2_free[0].wait() + PO2_free[1].wait() if s == 0 else []),
                     inc=[s_w2m.inc()])
                w2qs_free[wb].freed_by((s_w2m, s_w2m.v))
            if debug:
                hq_free.freed_by((s_w2m, s_w2m.v), (s_dbg3, s_dbg3.v))
            else:
                hq_free.freed_by((s_w2m, s_w2m.v))
            t_w2m = (s_w2m, s_w2m.v)
            for tl in range(2):
                i = 2 * m + tl
                mb = m % 2
                p.op("vector", lambda v, tl=tl: v.tensor_scalar(
                    out=ffnb[:, :], in0=PO2[tl][:, :], scalar1=fac2c[:, tl:tl + 1],
                    scalar2=None, op0=OP.mult),
                    wait=[t_w2m] + ffnb_free.wait(), inc=[s_fev.inc()])
                if tl == 1:
                    PO2_free[0].freed_by((s_fev, s_fev.v))
                    PO2_free[1].freed_by((s_fev, s_fev.v))
                    fac2_free.freed_by((s_fev, s_fev.v))
                t_frs, (r_ap, m_ap) = emit_ln(
                    i, lambda lo, hi: ffnb[:, lo:hi],
                    (s_fst, s_fsq, s_frs),
                    extra_wait_bn=[(s_fev, s_fev.v)] + stats_free[i % 4].wait())
                p.op("scalar", lambda sE, r=r_ap, m_=m_ap: sE.activation(
                    out=ffnb[:, :], in_=ffnb[:, :], func=AF.Identity, bias=m_, scale=r),
                    wait=[t_frs], inc=[s_fap.inc()])
                stats_free[i % 4].freed_by((s_fap, s_fap.v))
                p.op("vector", lambda v, tl=tl, mb=mb: v.tensor_tensor(
                    out=x2in[:, mb, tl, :], in0=x2in[:, mb, tl, :], in1=ffnb[:, :],
                    op=OP.add),
                    wait=[(s_fap, s_fap.v)], inc=[s_ow.inc()])
                ffnb_free.freed_by((s_ow, s_ow.v))
                p.op("gpsimd", lambda g, i=i, mb=mb, tl=tl: g.dma_start(
                    out=out_d[i * 128:(i + 1) * 128, :], in_=x2in[:, mb, tl, :]),
                    wait=[(s_ow, s_ow.v)], inc=[s_ost3.inc(16)])
                if tl == 1:
                    x2in_free[mb].freed_by((s_ost3, s_ost3.v))

        PO2_free = [Res(), Res()]

        for k in range(NCH):
            buf = k % 2
            slot = HS[k % 3]
            p.op("sync", lambda e, k=k, buf=buf: e.dma_start(
                out=x2in[:, buf, :, :],
                in_=x2_d[k * 256:(k + 1) * 256, :].rearrange("(t p) d -> p t d", p=128)),
                wait=x2in_free[buf].wait() + [(s_x2s[0], 16 * (k + 1)), (s_x2s[1], 16 * (k + 1))],
                inc=[s_x3[buf].inc(16)])
            w1toks = {}

            def emit_w1_load(b, k=k):
                wb = b % 2
                p.op("sync", lambda e, b=b, wb=wb: e.dma_start(
                    out=w1qs[:, wb, :, :, :], in_=w1q_d[4 * b:4 * (b + 1)].transpose([2, 0, 1, 3])),
                    wait=w1qs_free[wb].wait() + ([t_wq_stored0, t_wq_stored1] if k == 0 and b < 2 else []),
                    inc=[s_w1l[wb].inc(16)])
                w1toks[b] = (s_w1l[wb], s_w1l[wb].v)

            emit_w1_load(0)
            emit_w1_load(1)
            for tl in range(2):
                p3_lnquant(k, tl)
            t_xqT = (s_xte, s_xte.v)
            t_inv = (s_inv, s_inv.v)
            for s in range(32):
                b = s // 4
                pb = ph[s % 2]
                def w1mm(t, s=s, buf=buf, pb=pb):
                    last = None
                    for c in range(8):
                        last = t.matmul(out=pb[:, :], lhsT=w1qs[:, (s // 4) % 2, s % 4, c, :],
                                        rhs=xqT[:, buf, c, :], start=(c == 0), stop=(c == 7))
                    return last
                p.op("tensor", w1mm,
                     wait=[w1toks[b], t_xqT] + ph_free[s % 2].wait(), inc=[s_hmm.inc()])
                if s % 4 == 3:
                    w1qs_free[(s // 4) % 2].freed_by((s_hmm, s_hmm.v))
                    if b + 2 < 8:
                        emit_w1_load(b + 2)
                p.op("vector", lambda v, s=s, pb=pb, slot=slot, buf=buf: v.tensor_tensor(
                    out=slot[:, s, 1:257], in0=pb[:, :], in1=invB[:, buf, :], op=OP.mult),
                    wait=[(s_hmm, s_hmm.v), t_inv] +
                    (slot_free[k % 3].wait() if s == 0 else []),
                    inc=[s_hev3.inc()])
                ph_free[s % 2].freed_by((s_hev3, s_hev3.v))
            xqT_free[buf].freed_by((s_hmm, s_hmm.v))
            inv_free[buf].freed_by((s_hev3, s_hev3.v))
            t_hv = (s_hev3, s_hev3.v)
            if k >= 1:
                prev = HS[(k - 1) % 3]
                p.op("vector", lambda v, prev=prev, slot=slot: v.tensor_copy(
                    out=prev[:, :, 257:258], in_=slot[:, :, 1:2]),
                    wait=[t_hv], inc=[s_stash.inc()])
                p.op("vector", lambda v, prev=prev, slot=slot: v.tensor_copy(
                    out=slot[:, :, 0:1], in_=prev[:, :, 256:257]),
                    inc=[s_stash.inc()])
            else:
                p.op("vector", lambda v, slot=slot: v.memset(slot[:, :, 0:1], 0.0),
                     wait=[t_hv], inc=[s_stash.inc(2)])
            if k >= 1:
                p3_mid(k - 1)
        lastslot = HS[(NCH - 1) % 3]
        p.op("vector", lambda v: v.memset(lastslot[:, :, 257:258], 0.0),
             wait=[(s_hev3, s_hev3.v)], inc=[s_stash.inc(2)])
        p3_mid(NCH - 1)

        for e in ENGINES:
            p.op(e, lambda eng: eng.nop(), wait=[(s_ost3, 16 * NT)])
        for e in ENGINES:
            p.op(e, lambda eng: eng.nop(), wait=[(s_xst[0], s_xst[0].v), (s_xst[1], s_xst[1].v)])
        for e in ENGINES:
            p.op(e, lambda eng: eng.nop(), wait=[(s_x2s[0], s_x2s[0].v), (s_x2s[1], s_x2s[1].v)])
        for e in ENGINES:
            p.op(e, lambda eng: eng.nop(), wait=[(s_qst[0], s_qst[0].v), (s_qst[1], s_qst[1].v)])
        if debug:
            for e in ENGINES:
                p.op(e, lambda eng: eng.nop(), wait=[(s_dbg3, s_dbg3.v)])

        es3.close()
        return _finish(nc, es_g, p)


def _finish(nc, es_g, p):
    block = es_g.enter_context(nc.Block())

    @block.sync
    def _(eng):
        p.replay("sync", eng)

    @block.vector
    def _(eng):
        p.replay("vector", eng)

    @block.scalar
    def _(eng):
        p.replay("scalar", eng)

    @block.tensor
    def _(eng):
        p.replay("tensor", eng)

    @block.gpsimd
    def _(eng):
        p.replay("gpsimd", eng)
    return nc


_NC_CACHE = {}


def kernel(L0_attn_x, L0_attn_Wq, L0_attn_bq, L0_attn_Wk, L0_attn_bk, L0_attn_Wv,
           L0_attn_bv, L0_attn_Wg, L0_attn_bg, L0_attn_Wo, L0_attn_bo,
           L0_ffn_w1, L0_ffn_conv_w, L0_ffn_conv_b, L0_ffn_w2):
    if "nc" not in _NC_CACHE:
        _NC_CACHE["nc"] = build_nc(debug=False)
    nc = _NC_CACHE["nc"]
    x = np.ascontiguousarray(L0_attn_x, dtype=np.float32)
    shared = {
        "wq": np.ascontiguousarray(L0_attn_Wq, np.float32),
        "wk": np.ascontiguousarray(L0_attn_Wk, np.float32),
        "wv": np.ascontiguousarray(L0_attn_Wv, np.float32),
        "wg": np.ascontiguousarray(L0_attn_Wg, np.float32),
        "wo": np.ascontiguousarray(L0_attn_Wo, np.float32),
        "bq": np.ascontiguousarray(L0_attn_bq, np.float32),
        "bk": np.ascontiguousarray(L0_attn_bk, np.float32),
        "bv": np.ascontiguousarray(L0_attn_bv, np.float32),
        "bg": np.ascontiguousarray(L0_attn_bg, np.float32),
        "bo": np.ascontiguousarray(L0_attn_bo, np.float32),
        "w1": np.ascontiguousarray(L0_ffn_w1, np.float32),
        "cw": np.ascontiguousarray(L0_ffn_conv_w, np.float32).reshape(DH, 3),
        "cb": np.ascontiguousarray(L0_ffn_conv_b, np.float32),
        "w2": np.ascontiguousarray(L0_ffn_w2, np.float32),
    }
    in_maps = [dict(shared, x=np.ascontiguousarray(x[c])) for c in range(8)]
    res = run_bass_kernel_spmd(nc, in_maps, core_ids=list(range(8)))
    return np.stack([res.results[c]["out"] for c in range(8)], axis=0)
